# version 04
# baseline (speedup 1.0000x reference)
import math
from contextlib import ExitStack

import numpy as np
import concourse.bass as bass
import concourse.mybir as mybir
from concourse.bass_utils import run_bass_kernel_spmd
from concourse.alu_op_type import AluOpType as Op

F32 = mybir.dt.float32
BF16 = mybir.dt.bfloat16
I32 = mybir.dt.int32
AF = mybir.ActivationFunctionType
AX = mybir.AxisListType.X

NCORE = 8
NSEQ = 2
T = 4096
D = 1024
NCH = D // 128
SL = 512
NSL = T // SL
ENGINES = ("tensor", "vector", "scalar", "gpsimd", "sync")

_STDS = [1.0, 2.0, 4.0, 8.0]
def _chunks_for(std):
    cut = std * math.sqrt(1490.0)
    lo = -cut - std
    hi = 511.0 + cut - std
    c0 = int(math.floor(lo / 128.0)) * 128
    c1 = int(math.floor(hi / 128.0)) * 128
    return list(range(c0, c1 + 1, 128))
CHUNKS = [_chunks_for(s) for s in _STDS]
PROBIDX = {}
_n = 0
for _si in range(4):
    for _c in CHUNKS[_si]:
        PROBIDX[(_si, _c)] = _n
        _n += 1
NPROBS = _n

LN_EPS = 1e-5
RMS_EPS = 1e-6
C_LN_EPS = 1e-6
C_Z_EPS = 1e-6
GELU_C = 0.044715
GELU_S = math.sqrt(2.0 / math.pi)



class Env:
    def __init__(self, nc, stream_kinds=None):
        self.nc = nc
        self.stack = ExitStack()
        self.steps = {e: [] for e in ENGINES}
        self.semh = {}
        self.cnt = {}
        self.dram = {}
        self.needed_inputs = []
        self.output_names = []
        self.stream_kinds = stream_kinds or {}
        self.nblocks = 0
        self.conv_tok = {}
        self.debug = False

    def tap(self, name, ap, waits=()):
        if not self.debug or ("dbg_" + name) in self.dram:
            return
        d = self.dout("dbg_" + name, list(ap.shape), ap.dtype)
        self.dma("sync", d[tuple(slice(None) for _ in ap.shape)], ap, waits, ch="st")

    def sem(self, name):
        if name not in self.semh:
            self.semh[name] = self.stack.enter_context(self.nc.semaphore(name))
            self.cnt[name] = 0
        return self.semh[name]

    def val(self, name):
        self.sem(name)
        return self.cnt[name]

    def tok(self, name):
        return (name, self.val(name))

    def rec(self, eng, fn, waits=(), inc=None, k=1):
        if inc is not None:
            self.sem(inc)
        waits = [w for w in waits if w is not None and w[1] > 0]
        for s, v in waits:
            self.sem(s)
            assert v <= self.cnt[s], f"wait on future value {s}>={v} (now {self.cnt[s]})"
        if inc is not None:
            self.cnt[inc] += k
        self.steps[eng].append((fn, waits, inc, k))
        return (inc, self.cnt[inc]) if inc is not None else None

    def wait_only(self, eng, waits):
        self.rec(eng, None, waits)

    def dma(self, q, out, in_, waits=(), ch="ld"):
        return self.rec(q, lambda e, o=out, i=in_: e.dma_start(out=o, in_=i),
                        waits, inc=ch, k=16)

    def flush(self, name=""):
        steps, self.steps = self.steps, {e: [] for e in ENGINES}
        if not any(steps.values()):
            return
        self.nblocks += 1
        semh = self.semh
        with self.nc.named_scope(name or f"blk{self.nblocks}"), \
                self.nc.Block() as block:
            for eng in ENGINES:
                lst = steps[eng]
                if not lst:
                    continue

                def make(lst):
                    def section(proxy):
                        for fn, waits, inc, k in lst:
                            for s, v in waits:
                                proxy.wait_ge(semh[s], v)
                            if fn is None:
                                continue
                            instr = fn(proxy)
                            if inc is not None:
                                assert instr is not None, "inc step must return instr"
                                instr.then_inc(semh[inc], k)
                    return section

                getattr(block, eng)(make(lst))

    def din(self, name, shape, dtype=F32):
        if name not in self.dram:
            self.dram[name] = self.nc.dram_tensor(name, list(shape), dtype,
                                                  kind="ExternalInput")
            self.needed_inputs.append(name)
        return self.dram[name]

    def dout(self, name, shape, dtype=F32):
        if name not in self.dram:
            self.dram[name] = self.nc.dram_tensor(name, list(shape), dtype,
                                                  kind="ExternalOutput")
            self.output_names.append(name)
        return self.dram[name]

    def dtmp(self, name, shape, dtype=F32):
        if name not in self.dram:
            self.dram[name] = self.nc.dram_tensor(name, list(shape), dtype)
        return self.dram[name]

    def stream(self, name):
        kind = self.stream_kinds.get(name)
        if kind == "in":
            return self.din(name, [NSEQ, D, T])
        if kind == "out":
            return self.dout(name, [NSEQ, D, T])
        return self.dtmp(name, [NSEQ, D, T])


WEIGHTS_2D = {
    "winT": (1024, 1024),
    "woutT": (1024, 1024),
    "b_wq": (1024, 1024),
    "b_wdown": (1024, 64),
    "b_wup": (64, 2048),
    "b_wo": (1024, 1024),
    "b_we": (8 * 1024, 1024),
    "c_wq": (8 * 1024, 1024),
    "c_wk": (8 * 1024, 1024),
    "c_wv": (8 * 1024, 1024),
    "c_wo": (8 * 1024, 1024),
    "c_wf1": (8 * 1024, 4096),
    "c_wf2": (8 * 4096, 1024),
    "d_w1": (8 * 1024, 2048),
    "d_w2": (8 * 1024, 2048),
    "d_w3": (8 * 2048, 1024),
}

CONV_NEED = {
    "A": ["winT", "woutT"],
    "B1": ["b_wq", "b_wdown", "b_wup", "b_wo"],
    "B2": ["b_we"],
    "C": ["c_wq", "c_wk", "c_wv", "c_wo", "c_wf1", "c_wf2"],
    "D": ["d_w1", "d_w2", "d_w3"],
}


class Bank:

    def __init__(self, handle):
        self.h = handle
        self.tok = None


def emit_convert(env, names):
    nc = env.nc
    env.conv_tok = {}
    with ExitStack() as st:
        sin = [st.enter_context(nc.sbuf_tensor(f"cv_i{i}", [128, 4096], F32))
               for i in range(2)]
        sout = [st.enter_context(nc.sbuf_tensor(f"cv_o{i}", [128, 4096], BF16))
                for i in range(2)]
        tiles = []
        for name in names:
            R, C = WEIGHTS_2D[name]
            src = env.din(name, [R, C])
            dst = env.dtmp("bf_" + name, [R, C], BF16)
            for r0 in range(0, R, 128):
                tiles.append((name, src, dst, r0, min(128, R - r0), C))
        toks = {}
        for i, (name, src, dst, r0, rr, C) in enumerate(tiles):
            b = i % 2
            ldw = [toks.get(("cast", i - 2))]
            tld = env.dma("sync", sin[b][:rr, :C], src[r0:r0 + rr, :], ldw, ch="cv_ld")
            cw = [tld, toks.get(("st", i - 2))]
            toks[("cast", i)] = env.rec(
                "vector",
                lambda e, b=b, rr=rr, C=C: e.tensor_copy(sout[b][:rr, :C], sin[b][:rr, :C]),
                cw, inc="cv_dve")
            toks[("st", i)] = env.dma("gpsimd", dst[r0:r0 + rr, :], sout[b][:rr, :C],
                                      [toks[("cast", i)]], ch="cv_st")
            env.conv_tok[name] = toks[("st", i)]
        env.flush("convert")



def emit_rsqrt(env, w_ap, out_ap, ybi_ap, t1_ap, waits=(), inc=None, n_iter=3,
               chain=False):
    r = env.rec
    seq = [
        lambda e: e.tensor_scalar(ybi_ap, w_ap.bitcast(I32), 1, None,
                                  op0=Op.logical_shift_right),
        lambda e: e.tensor_scalar(ybi_ap, ybi_ap, -1, 0x5F3759DF,
                                  op0=Op.mult, op1=Op.add),
        lambda e: e.tensor_copy(out_ap, ybi_ap.bitcast(F32)),
    ]
    for it in range(n_iter):
        seq += [
            lambda e: e.tensor_tensor(t1_ap, out_ap, out_ap, op=Op.mult),
            lambda e: e.tensor_tensor(t1_ap, t1_ap, w_ap, op=Op.mult),
            lambda e: e.tensor_scalar(t1_ap, t1_ap, -0.5, 1.5,
                                      op0=Op.mult, op1=Op.add),
            lambda e: e.tensor_tensor(out_ap, out_ap, t1_ap, op=Op.mult),
        ]
    tok = None
    for i, fn in enumerate(seq):
        w = list(waits) if i == 0 else ([tok] if chain else [])
        want_inc = "dve" if (chain or i == len(seq) - 1) else None
        if i == len(seq) - 1 and inc is not None:
            want_inc = inc
        tok = r("vector", fn, w, inc=want_inc)
    return tok


def emit_identity(env, ident_ap, fill=1.0):
    n = ident_ap.shape[0]
    env.rec("gpsimd", lambda e: e.memset(ident_ap, 0.0))
    return env.rec("gpsimd", lambda e: e.affine_select(
        out=ident_ap, in_=ident_ap, compare_op=Op.not_equal, fill=fill,
        base=0, pattern=[[-1, n]], channel_multiplier=1), inc="pool")


def emit_globals(env):
    nc = env.nc
    g = lambda nm, shp, dt: env.stack.enter_context(nc.sbuf_tensor(nm, shp, dt))
    env.identb = g("g_identb", [128, 128], BF16)
    env.identf = g("g_identf", [128, 128], F32)
    env.onescol = g("g_onescol", [128, 1], BF16)
    env.onesrow = g("g_onesrow", [1, 128], F32)
    env.vtab = g("g_vtab", [128, VROWS], F32)
    emit_identity(env, env.identb[:, :])
    t_id = emit_identity(env, env.identf[:, :])
    env.rec("gpsimd", lambda e: e.memset(env.onescol[:, :], 1.0))
    t_ones = env.rec("gpsimd", lambda e: e.memset(env.onesrow[:, :], 1.0), inc="pool")
    env.t_globals_pool = t_ones
    vecs = env.din("vecs", [VROWS, 128])
    with ExitStack() as st:
        stg = st.enter_context(nc.sbuf_tensor("g_vstg", [128, 2, 128], F32))
        pvt = st.enter_context(nc.psum_tensor("g_pvt", [128, 128], F32))
        tprev = None
        for i in range(VROWS // 128):
            b = i % 2
            tld = env.dma("sync", stg[:, b, :], vecs[i * 128:(i + 1) * 128, :],
                          [tprev], ch="ld")
            tt = env.rec("tensor", lambda e, b=b: e.transpose(
                pvt[:, :], stg[:, b, :], env.identf[:, :]),
                [tld, t_id if i == 0 else None, tprev], inc="pe")
            tprev = env.rec("vector", lambda e, i=i: e.tensor_copy(
                env.vtab[:, i * 128:(i + 1) * 128], pvt[:, :]), [tt], inc="dve")
        env.flush("globals")


def vcol(env, name, chunk=0):
    r = VIDX[name] + chunk
    return env.vtab[:, r:r + 1]



def emit_stageA(env):
    nc = env.nc
    x0 = env.din("x0", [NSEQ, T, D])
    S0 = env.stream("S0")
    wbfi = env.dram["bf_winT"]
    wbfo = env.dram["bf_woutT"]
    ar = env.din("arange2d", [128, SL])

    with ExitStack() as st:
        sb = lambda nm, shp, dt=F32: st.enter_context(nc.sbuf_tensor(nm, shp, dt))
        ps = lambda nm, shp, dt=F32: st.enter_context(nc.psum_tensor(nm, shp, dt))

        identb = env.identb
        identf = env.identf
        win_sb = sb("win_sb", [128, NCH, D], BF16)
        wout_sb = sb("wout_sb", [128, NCH, D], BF16)
        vT = sb("vT", [128, T // 128, D], BF16)
        probs = sb("probs", [128, NPROBS, SL], BF16)
        st_set = ExitStack()
        sbs = lambda nm, shp, dt=F32: st_set.enter_context(nc.sbuf_tensor(nm, shp, dt))
        ar_sb = sbs("ar_sb", [128, SL], F32)
        ptmp = [sbs(f"ptmp{i}", [128, SL], F32) for i in range(3)]

        t_id = env.t_globals_pool
        t_ar = env.dma("sync", ar_sb[:, :], ar[:, :])
        cvw = [env.conv_tok.get("winT"), env.conv_tok.get("woutT")]
        for k in range(NCH):
            env.dma("sync", win_sb[:, k, :], wbfi[k * 128:(k + 1) * 128, :],
                    cvw if k == 0 else ())
            env.dma("sync", wout_sb[:, k, :], wbfo[k * 128:(k + 1) * 128, :])
        t_w = env.tok("ld")
        t_sq_prev = None
        t_fin_prev = None
        for si in range(4):
            std = _STDS[si]
            inv_norm = 1.0 / (std * math.sqrt(2.0 * math.pi))
            for c in CHUNKS[si]:
                idx = PROBIDX[(si, c)]
                tpre = env.rec("vector", lambda e, c=c, std=std: e.tensor_scalar(
                    ptmp[0][:, :], ar_sb[:, :], float(c + std), 1.0 / std,
                    op0=Op.add, op1=Op.mult), [t_ar, t_sq_prev], inc="dve")
                t_sq_prev = env.rec("scalar", lambda e: e.activation(
                    out=ptmp[1][:, :], in_=ptmp[0][:, :], func=AF.Square),
                    [tpre], inc="act")
                texp = env.rec("scalar", lambda e: e.activation(
                    out=ptmp[2][:, :], in_=ptmp[1][:, :], func=AF.Exp, scale=-0.5),
                    [t_fin_prev], inc="act")
                t_fin_prev = env.rec("vector", lambda e, idx=idx, inv=inv_norm:
                                     e.tensor_scalar(probs[:, idx, :], ptmp[2][:, :],
                                                     inv, None, op0=Op.mult),
                                     [texp], inc="dve")
        env.tap("probs", probs[:, :, :], [t_fin_prev])
        env.tap("win", win_sb[:, :, :], [t_w])
        env.flush("A_setup")
        st_set.close()

        for n in range(NSEQ):
            stA1 = ExitStack()
            sb1 = lambda nm, shp, dt=F32: stA1.enter_context(nc.sbuf_tensor(nm, shp, dt))
            ps1 = lambda nm, shp, dt: stA1.enter_context(nc.psum_tensor(nm, shp, dt))
            xg = sb1(f"xg{n}", [128, 4, D], F32)
            ybuf = sb1(f"ybuf{n}", [128, 4, D], BF16)
            yT = sb1(f"yT{n}", [128, NCH, SL], BF16)
            junk = sb1(f"junkA{n}", [128, D], F32)
            stats = sb1(f"statsA{n}", [128, 16], F32)
            nwt_i = sb1(f"nwtAi{n}", [128, 4], I32)
            nwt_t = sb1(f"nwtAt{n}", [128, 4], F32)
            rstd = sb1(f"rstdA{n}", [128, 4], F32)
            vb = sb1(f"vbA{n}", [128, 2, SL], BF16)
            pT = [Bank(ps1(f"ApT{n}{i}", [128, 8, 128], BF16)) for i in range(2)]
            pv = [Bank(ps1(f"Apv{n}{i}", [128, SL], F32)) for i in range(2)]
            pvt = [Bank(ps1(f"Apvt{n}{i}", [128, 4, 128], BF16)) for i in range(2)]
            t_apl_prev = None
            t_ytr_prev = None
            t_vmm_prev = None
            for g in range(NSL):
                t0 = g * SL
                tld = env.dma("sync",
                              xg[:, :, :],
                              x0[n, t0:t0 + SL, :].rearrange("(f p) d -> p f d", p=128),
                              [t_apl_prev], ch="ld")
                tacc = None
                for f in range(4):
                    env.rec("vector", lambda e, f=f: e.reduce_sum(
                        stats[:, f:f + 1], xg[:, f, :], axis=AX),
                        [tld] if f == 0 else ())
                    tacc = env.rec("vector", lambda e, f=f: e.scalar_tensor_tensor(
                        junk[:, :], xg[:, f, :], 0.0, xg[:, f, :],
                        op0=Op.bypass, op1=Op.mult, accum_out=stats[:, 4 + f:5 + f]),
                        inc="dve" if f == 3 else None)
                t1 = env.rec("vector", lambda e: e.tensor_scalar(
                    stats[:, 8:12], stats[:, 0:4], 1.0 / D, None, op0=Op.mult),
                    [tacc], inc="dve")
                t2 = env.rec("vector", lambda e: e.tensor_scalar(
                    stats[:, 12:16], stats[:, 4:8], 1.0 / D, LN_EPS,
                    op0=Op.mult, op1=Op.add), [t1], inc="dve")
                t3 = env.rec("vector", lambda e: e.tensor_tensor(
                    stats[:, 0:4], stats[:, 8:12], stats[:, 8:12], op=Op.mult),
                    [t2], inc="dve")
                t4 = env.rec("vector", lambda e: e.tensor_tensor(
                    stats[:, 4:8], stats[:, 12:16], stats[:, 0:4], op=Op.subtract),
                    [t3], inc="dve")
                tnw = emit_rsqrt(env, stats[:, 4:8], rstd[:, :], nwt_i[:, :],
                                 nwt_t[:, :], waits=[t4], inc="dve", chain=True)
                tap = None
                for f in range(4):
                    tap = env.rec("vector", lambda e, f=f: e.tensor_scalar(
                        ybuf[:, f, :], xg[:, f, :], stats[:, 8 + f:9 + f],
                        rstd[:, f:f + 1], op0=Op.subtract, op1=Op.mult),
                        [tnw, t_ytr_prev] if f == 0 else (),
                        inc="dve" if f == 3 else None)
                t_apl_prev = tap
                for f in range(4):
                    bank = pT[f % 2]
                    tw = [tap if f == 0 else None,
                          t_vmm_prev if f == 0 else None,
                          bank.tok]
                    tt = None
                    for c in range(NCH):
                        tt = env.rec("tensor", lambda e, bank=bank, c=c, f=f: e.transpose(
                            bank.h[:, c, :], ybuf[:, f, c * 128:(c + 1) * 128],
                            identb[:, :]),
                            tw + ([t_id] if (n == 0 and g == 0 and f == 0) else [])
                            if c == 0 else (),
                            inc="pe" if c == NCH - 1 else None)
                    if f == 3:
                        t_ytr_prev = tt
                    bank.tok = env.rec(
                        "scalar", lambda e, bank=bank, f=f: e.copy(
                            yT[:, :, f * 128:(f + 1) * 128], bank.h[:, :, :]),
                        [tt], inc="act")
                t_yT = pT[1].tok
                for m in range(NCH):
                    bank = pv[m % 2]
                    mw = [t_yT if m == 0 else None,
                          t_w if (n == 0 and g == 0 and m == 0) else None,
                          bank.tok]
                    tmm = None
                    for k in range(NCH):
                        tmm = env.rec("tensor", lambda e, bank=bank, k=k, m=m: e.matmul(
                            bank.h[:, :], win_sb[:, k, m * 128:(m + 1) * 128],
                            yT[:, k, :], start=(k == 0), stop=(k == NCH - 1)),
                            mw if k == 0 else (), inc="pe" if k == NCH - 1 else None)
                    if m == NCH - 1:
                        t_vmm_prev = tmm
                    tvev = env.rec("scalar", lambda e, bank=bank, m=m: e.copy(
                        vb[:, m % 2, :], bank.h[:, :]),
                        [tmm, pvt[m % 2].tok], inc="act")
                    bank.tok = tvev
                    tbk = pvt[m % 2]
                    t2 = None
                    for j in range(4):
                        t2 = env.rec("tensor", lambda e, tbk=tbk, j=j, m=m: e.transpose(
                            tbk.h[:, j, :], vb[:, m % 2, j * 128:(j + 1) * 128],
                            identb[:, :]),
                            [tvev, tbk.tok] if j == 0 else (),
                            inc="pe" if j == 3 else None)
                    tbk.tok = env.rec(
                        "vector", lambda e, tbk=tbk, g=g, m=m: e.tensor_copy(
                            vT[:, g * 4:g * 4 + 4, m * 128:(m + 1) * 128], tbk.h[:, :, :]),
                        [t2], inc="dve")
            if n == 0:
                env.tap("vT", vT[:, :, :], [pvt[1].tok])
                env.tap("ybuf", ybuf[:, :, :], [t_apl_prev])
                env.tap("stats", stats[:, :], [t_apl_prev])
                env.tap("rstd", rstd[:, :], [t_apl_prev])
                env.tap("xg", xg[:, :, :], [t_apl_prev])
                env.tap("yT", yT[:, :, :], [pT[1].tok])
                env.tap("vb", vb[:, :, :], [pv[1].tok])
            env.flush(f"A1_n{n}")
            stA1.close()

            stA2 = ExitStack()
            sb2 = lambda nm, shp, dt=F32: stA2.enter_context(nc.sbuf_tensor(nm, shp, dt))
            ps2 = lambda nm, shp, dt: stA2.enter_context(nc.psum_tensor(nm, shp, dt))
            xs = sb2(f"xsA{n}", [128, 4, D], F32)
            x0T = sb2(f"x0TA{n}", [128, NCH, SL], F32)
            attb = sb2(f"attbA{n}", [128, NCH, SL], BF16)
            x1_sb = sb2(f"x1A{n}", [128, NCH, SL], F32)
            pa = [Bank(ps2(f"Apa{n}{i}", [128, SL], F32)) for i in range(2)]
            po = [Bank(ps2(f"Apo{n}{i}", [128, SL], F32)) for i in range(2)]
            pxT = [Bank(ps2(f"ApxT{n}{i}", [128, 4, 128], F32)) for i in range(2)]
            t_xtr_prev = None
            t_womm_prev = None
            t_st_prev = None
            x0T_tok = [None] * NCH
            for s in range(NSL):
                q0 = s * SL
                tld = env.dma("sync",
                              xs[:, :, :],
                              x0[n, q0:q0 + SL, :].rearrange("(f p) d -> p f d", p=128),
                              [t_xtr_prev], ch="ld")
                for c in range(NCH):
                    bank = pxT[c % 2]
                    tt = None
                    for f in range(4):
                        tt = env.rec("tensor", lambda e, bank=bank, c=c, f=f: e.transpose(
                            bank.h[:, f, :], xs[:, f, c * 128:(c + 1) * 128],
                            identf[:, :]),
                            [tld, bank.tok] if f == 0 else (),
                            inc="pe" if f == 3 else None)
                    if c == NCH - 1:
                        t_xtr_prev = tt
                    bank.tok = env.rec(
                        "vector", lambda e, bank=bank, c=c: e.tensor_copy(
                            x0T[:, c, :], bank.h[:, :, :]), [tt], inc="dve")
                    x0T_tok[c] = bank.tok
                for h in range(NCH):
                    si = h % 4
                    cl = [c for c in CHUNKS[si] if 0 <= q0 + c < T]
                    bank = pa[h % 2]
                    tmm = None
                    for ci, c in enumerate(cl):
                        kt = (q0 + c) // 128
                        tmm = env.rec("tensor", lambda e, bank=bank, kt=kt, h=h,
                                      c=c, si=si, st_=(ci == 0),
                                      sp=(ci == len(cl) - 1):
                                      e.matmul(bank.h[:, :],
                                               vT[:, kt, h * 128:(h + 1) * 128],
                                               probs[:, PROBIDX[(si, c)], :],
                                               start=st_, stop=sp),
                                      [bank.tok] if ci == 0 else (),
                                      inc="pe" if ci == len(cl) - 1 else None)
                    bank.tok = env.rec(
                        "scalar", lambda e, bank=bank, h=h: e.copy(
                            attb[:, h, :], bank.h[:, :]),
                        [tmm, t_womm_prev if h == 0 else None], inc="act")
                t_attb = pa[(NCH - 1) % 2].tok
                for m in range(NCH):
                    bank = po[m % 2]
                    mw = [t_attb if m == 0 else None, bank.tok]
                    tmm = None
                    for k in range(NCH):
                        tmm = env.rec("tensor", lambda e, bank=bank, k=k, m=m: e.matmul(
                            bank.h[:, :], wout_sb[:, k, m * 128:(m + 1) * 128],
                            attb[:, k, :], start=(k == 0), stop=(k == NCH - 1)),
                            mw if k == 0 else (), inc="pe" if k == NCH - 1 else None)
                    if m == NCH - 1:
                        t_womm_prev = tmm
                    bank.tok = env.rec(
                        "vector", lambda e, bank=bank, m=m: e.tensor_tensor(
                            x1_sb[:, m, :], bank.h[:, :], x0T[:, m, :], op=Op.add),
                        [tmm, x0T_tok[m],
                         t_st_prev if m == 0 else None],
                        inc="dve")
                t_st_prev = env.dma(
                    "scalar",
                    S0[n].rearrange("(c p) t -> p c t", p=128)[:, :, q0:q0 + SL],
                    x1_sb[:, :, :], [po[(NCH - 1) % 2].tok], ch="st")
            env.flush(f"A2_n{n}")
            stA2.close()
    return env.tok("st")



def _bcast(env, row_ap, psum_ap, waits):
    return env.rec("tensor", lambda e: e.matmul(
        psum_ap, env.onesrow[:1, :], row_ap, start=True, stop=True),
        waits, inc="pe")


def emit_stageB1(env):
    nc = env.nc
    S0 = env.stream("S0")
    S1 = env.stream("S1")
    NTOK = NSEQ * T
    with ExitStack() as st:
        sb = lambda nm, shp, dt=F32: st.enter_context(nc.sbuf_tensor(nm, shp, dt))
        ps = lambda nm, shp, dt=F32: st.enter_context(nc.psum_tensor(nm, shp, dt))
        wq_sb = sb("b1_wq", [128, NCH, D], BF16)
        wdn_sb = sb("b1_wdn", [128, NCH, 64], BF16)
        wup_sb = sb("b1_wup", [64, 2048], BF16)
        wo_sb = sb("b1_wo", [128, NCH, D], BF16)
        x1s = sb("b1_x1", [128, NCH, SL], F32)
        xcb = sb("b1_xcb", [128, 2, SL], BF16)
        s2b = sb("b1_s2b", [128, 2, SL], BF16)
        y = sb("b1_y", [128, NCH, SL], F32)
        h = sb("b1_h", [128, NCH, SL], BF16)
        qb = sb("b1_qb", [128, NCH, SL], BF16)
        q_tok = sb("b1_qtok", [128, 4, D], BF16)
        kdb = sb("b1_kdb", [64, SL], BF16)
        kvb = sb("b1_kvb", [128, 16, SL], BF16)
        kv_tok = sb("b1_kvtok", [128, 4, 2048], BF16)
        out_tok = sb("b1_otok", [128, 4, D], BF16)
        out_fm = sb("b1_ofm", [128, NCH, SL], BF16)
        rows = sb("b1_rows", [1, 8, SL], F32)
        nwt_i = sb("b1_nwi", [1, SL], I32)
        esc = {}
        for eng, tag in (("vector", "v"), ("gpsimd", "p")):
            esc[eng] = dict(
                prod=sb(f"b1_prod{tag}", [128, 4, 16, 64], BF16),
                tree=sb(f"b1_tree{tag}", [128, 2048], F32),
                att=sb(f"b1_att{tag}", [128, 16, 16], F32),
                eall=sb(f"b1_e{tag}", [128, 16, 16], F32),
                sm=sb(f"b1_sm{tag}", [128, 3, 16], F32),
            )
        prow_s = Bank(ps("b1_prs", [1, SL], F32))
        prow_q = Bank(ps("b1_prq", [1, SL], F32))
        pmm = [Bank(ps(f"b1_pmm{i}", [128, SL], F32)) for i in range(2)]
        ptr = [Bank(ps(f"b1_ptr{i}", [128, 8, 128], BF16)) for i in range(2)]
        pkd = Bank(ps("b1_pkd", [64, SL], F32))

        for k in range(NCH):
            env.dma("sync", wq_sb[:, k, :],
                    env.dram["bf_b_wq"][k * 128:(k + 1) * 128, :],
                    [env.conv_tok.get("b_wq")] if k == 0 else ())
            env.dma("sync", wdn_sb[:, k, :],
                    env.dram["bf_b_wdown"][k * 128:(k + 1) * 128, :],
                    [env.conv_tok.get("b_wdown")] if k == 0 else ())
            env.dma("sync", wo_sb[:, k, :],
                    env.dram["bf_b_wo"][k * 128:(k + 1) * 128, :],
                    [env.conv_tok.get("b_wo")] if k == 0 else ())
        env.dma("sync", wup_sb[:, :], env.dram["bf_b_wup"][:, :],
                [env.conv_tok.get("b_wup")])
        t_w = env.tok("ld")

        R_MEAN, R_VAR, R_VEPS, R_RSTD, R_T1, R_T2, R_S12, R_SP = range(8)
        row = lambda r: rows[0:1, r, :]
        t_x1_free = None
        t_xb_free = None
        t_sqd_free = None
        t_rows_free = None
        t_y_free = None
        t_qb_free = None
        t_kvb_free = None
        t_kdb_free = None
        t_h_free = None
        t_qtok_free = None
        t_otok_free = None

        for s in range(NTOK // SL):
            n, q0 = divmod(s * SL, T)
            tld = env.dma("sync", x1s[:, :, :],
                          S0[n].rearrange("(c p) t -> p c t", p=128)[:, :, q0:q0 + SL],
                          [t_x1_free], ch="ld")
            tsum = None
            for c in range(NCH):
                tcast = env.rec("vector", lambda e, c=c: e.tensor_copy(
                    xcb[:, c % 2, :], x1s[:, c, :]),
                    ([tld, t_xb_free] if c == 0 else []) +
                    ([tsum] if c >= 2 else []), inc="dve")
                tsum = env.rec("tensor", lambda e, c=c: e.matmul(
                    prow_s.h[:1, :], env.onescol[:, :], xcb[:, c % 2, :],
                    start=(c == 0), stop=(c == NCH - 1)),
                    [tcast] + ([prow_s.tok, t_w if s == 0 else None]
                               if c == 0 else []),
                    inc="pe")
            tsq = None
            for c in range(NCH):
                tsqd = env.rec("vector", lambda e, c=c: e.tensor_tensor(
                    s2b[:, c % 2, :], x1s[:, c, :], x1s[:, c, :], op=Op.mult),
                    [tsq] if c >= 2 else (), inc="dve")
                tsq = env.rec("tensor", lambda e, c=c: e.matmul(
                    prow_q.h[:1, :], env.onescol[:, :], s2b[:, c % 2, :],
                    start=(c == 0), stop=(c == NCH - 1)),
                    [tsqd] + ([prow_q.tok] if c == 0 else []), inc="pe")
            t_xb_free = tsum
            env.rec("vector", lambda e: e.tensor_scalar(
                row(R_MEAN), prow_s.h[:1, :], 1.0 / D, None, op0=Op.mult),
                [tsum, tsq, t_rows_free])
            env.rec("vector", lambda e: e.tensor_tensor(
                row(R_T1), row(R_MEAN), row(R_MEAN), op=Op.mult))
            env.rec("vector", lambda e: e.tensor_scalar(
                row(R_VAR), prow_q.h[:1, :], 1.0 / D, None, op0=Op.mult))
            tvar = env.rec("vector", lambda e: e.tensor_tensor(
                row(R_VAR), row(R_VAR), row(R_T1), op=Op.subtract), inc="dve")
            prow_s.tok = tvar
            prow_q.tok = tvar
            env.rec("vector", lambda e: e.tensor_scalar(
                row(R_VEPS), row(R_VAR), 1.0, LN_EPS, op0=Op.mult, op1=Op.add))
            emit_rsqrt(env, row(R_VEPS), row(R_RSTD), nwt_i[0:1, :], row(R_T1))
            env.rec("vector", lambda e: e.tensor_tensor(
                row(R_T1), row(R_RSTD), row(R_RSTD), op=Op.mult))
            env.rec("vector", lambda e: e.tensor_tensor(
                row(R_T2), row(R_T1), row(R_VAR), op=Op.mult))
            env.rec("vector", lambda e: e.tensor_scalar(
                row(R_T2), row(R_T2), 1.0, RMS_EPS, op0=Op.mult, op1=Op.add))
            emit_rsqrt(env, row(R_T2), row(R_SP), nwt_i[0:1, :], row(R_T1))
            trows = env.rec("vector", lambda e: e.tensor_tensor(
                row(R_S12), row(R_RSTD), row(R_SP), op=Op.mult), inc="dve")
            tb_m = _bcast(env, row(R_MEAN), pmm[0].h[:, :], [trows, pmm[0].tok])
            tb_r = _bcast(env, row(R_RSTD), pmm[1].h[:, :], [trows, pmm[1].tok])
            tsub = None
            for c in range(NCH):
                tsub = env.rec("vector", lambda e, c=c: e.tensor_tensor(
                    x1s[:, c, :], x1s[:, c, :], pmm[0].h[:, :], op=Op.subtract),
                    [tb_m] if c == 0 else (), inc="dve" if c == NCH - 1 else None)
            pmm[0].tok = tsub
            tb_s = _bcast(env, row(R_S12), pmm[0].h[:, :], [tsub, pmm[0].tok])
            t_rows_free = tb_s
            ty = None
            for c in range(NCH):
                ty = env.rec("vector", lambda e, c=c: e.tensor_tensor(
                    y[:, c, :], x1s[:, c, :], pmm[1].h[:, :], op=Op.mult),
                    [tb_r, t_y_free] if c == 0 else (),
                    inc="dve" if c == NCH - 1 else None)
            pmm[1].tok = ty
            th = None
            for c in range(NCH):
                th = env.rec("vector", lambda e, c=c: e.scalar_tensor_tensor(
                    h[:, c, :], x1s[:, c, :], vcol(env, "b_g1", c), pmm[0].h[:, :],
                    op0=Op.mult, op1=Op.mult),
                    [tb_s, t_h_free] if c == 0 else (),
                    inc="dve" if c == NCH - 1 else None)
            pmm[0].tok = th
            t_x1_free = th

            t_qev = None
            for m in range(NCH):
                bank = pmm[m % 2]
                tmm = None
                for k in range(NCH):
                    tmm = env.rec("tensor", lambda e, bank=bank, k=k, m=m: e.matmul(
                        bank.h[:, :], wq_sb[:, k, m * 128:(m + 1) * 128], h[:, k, :],
                        start=(k == 0), stop=(k == NCH - 1)),
                        ([th] if m == 0 else []) + [bank.tok] if k == 0 else (),
                        inc="pe" if k == NCH - 1 else None)
                t_qev = env.rec("scalar", lambda e, bank=bank, m=m: e.activation(
                    out=qb[:, m, :], in_=bank.h[:, :], func=AF.Identity,
                    bias=vcol(env, "b_bq", m)),
                    [tmm] + ([t_qb_free] if m == 0 else []), inc="act")
                bank.tok = t_qev
            tmm = None
            for k in range(NCH):
                tmm = env.rec("tensor", lambda e, k=k: e.matmul(
                    pkd.h[:, :], wdn_sb[:, k, :], h[:, k, :],
                    start=(k == 0), stop=(k == NCH - 1)),
                    [pkd.tok] if k == 0 else (), inc="pe" if k == NCH - 1 else None)
            tkd = env.rec("scalar", lambda e: e.activation(
                out=kdb[:, :], in_=pkd.h[:, :], func=AF.Identity,
                bias=env.vtab[0:64, VIDX["b_bdown"]:VIDX["b_bdown"] + 1]),
                [tmm, t_kdb_free], inc="act")
            pkd.tok = tkd
            tkvev = None
            t_kvmm = None
            for m in range(16):
                bank = pmm[m % 2]
                t_kvmm = env.rec("tensor", lambda e, bank=bank, m=m: e.matmul(
                    bank.h[:, :], wup_sb[0:64, m * 128:(m + 1) * 128], kdb[0:64, :],
                    start=True, stop=True),
                    ([tkd] if m == 0 else []) + [bank.tok], inc="pe")
                tkvev = env.rec("scalar", lambda e, bank=bank, m=m: e.activation(
                    out=kvb[:, m, :], in_=bank.h[:, :], func=AF.Identity,
                    bias=vcol(env, "b_bup", m)),
                    [t_kvmm] + ([t_kvb_free] if m == 0 else []), inc="act")
                bank.tok = tkvev
            t_kdb_free = t_kvmm
            t_qtok = [None] * 4
            t_kvtok = [None] * 4
            t_qtr_last = None
            for j in range(4):
                bank = ptr[j % 2]
                tt = None
                for m in range(NCH):
                    tt = env.rec("tensor", lambda e, bank=bank, j=j, m=m: e.transpose(
                        bank.h[:, m, :], qb[:, m, j * 128:(j + 1) * 128],
                        env.identb[:, :]),
                        [t_qev, bank.tok] if m == 0 else (),
                        inc="pe" if m == NCH - 1 else None)
                t_qtr_last = tt
                bank.tok = env.rec("vector", lambda e, bank=bank, j=j: e.tensor_copy(
                    q_tok[:, j, :], bank.h[:, :, :]),
                    [tt] + ([t_qtok_free] if j == 0 else []), inc="dve")
                t_qtok[j] = bank.tok
            t_qb_free = t_qtr_last
            t_kvtr_last = None
            for j in range(4):
                for half in range(2):
                    bank = ptr[(j * 2 + half) % 2]
                    tt = None
                    for mm_ in range(NCH):
                        m = half * 8 + mm_
                        tt = env.rec(
                            "tensor",
                            lambda e, bank=bank, j=j, m=m, mm_=mm_: e.transpose(
                                bank.h[:, mm_, :], kvb[:, m, j * 128:(j + 1) * 128],
                                env.identb[:, :]),
                            [tkvev, bank.tok] if mm_ == 0 else (),
                            inc="pe" if mm_ == NCH - 1 else None)
                    t_kvtr_last = tt
                    bank.tok = env.rec(
                        "vector", lambda e, bank=bank, j=j, half=half: e.tensor_copy(
                            kv_tok[:, j, half * 1024:(half + 1) * 1024],
                            bank.h[:, :, :]), [tt], inc="dve")
                    t_kvtok[j] = bank.tok
            t_kvb_free = t_kvtr_last

            def reduce_last(eng, ch, b, out_ap, in_ap, shape, op, waits):
                if eng == "vector":
                    red = {Op.add: "reduce_sum", Op.max: "reduce_max"}[op]
                    return env.rec(eng, lambda e: getattr(e, red)(
                        out_ap, in_ap, axis=AX), waits, inc=ch)
                W = shape[-1]
                lead = list(shape[1:-1])
                pre = 1
                for x in lead:
                    pre *= x
                cur = in_ap
                tok = None
                first = True
                while W > 1:
                    half = W // 2
                    if len(lead) == 2:
                        nxt = b["tree"][:, 0:pre * half].rearrange(
                            "p (x y w) -> p x y w", y=lead[1], w=half)
                    else:
                        nxt = b["tree"][:, 0:pre * half].rearrange(
                            "p (x w) -> p x w", w=half)
                    lo = tuple([slice(None)] * (1 + len(lead)) + [slice(0, half)])
                    hi = tuple([slice(None)] * (1 + len(lead)) +
                               [slice(half, 2 * half)])
                    is_last = (half == 1)
                    dst = out_ap.unsqueeze(1 + len(lead)) if is_last else nxt
                    tok = env.rec(eng, lambda e, cur=cur, dst=dst, lo=lo, hi=hi:
                                  e.tensor_tensor(dst, cur[lo], cur[hi], op=op),
                                  waits if first else (),
                                  inc=ch if is_last else None)
                    first = False
                    cur = nxt
                    W = half
                return tok

            def einsum_tile(j, eng):
                ch = "dve" if eng == "vector" else "pool"
                b = esc[eng]
                qv = q_tok[:, j, :].rearrange("p (h x) -> p h x", x=64)
                kv_ = kv_tok[:, j, :].rearrange("p (g x) -> p g x", x=128)
                kpart = kv_[:, :, 0:64]
                vpart = kv_[:, :, 64:128]
                tai = None
                for h4 in range(4):
                    qs = qv[:, h4 * 4:(h4 + 1) * 4, :].unsqueeze(2) \
                        .broadcast_to((128, 4, 16, 64))
                    ks = kpart.unsqueeze(1).broadcast_to((128, 4, 16, 64))
                    tp = env.rec(eng, lambda e, qs=qs, ks=ks: e.tensor_tensor(
                        b["prod"][:, :, :, :], qs, ks, op=Op.mult),
                        [t_qtok[j], t_kvtok[j], t_otok_free] if h4 == 0 else (),
                        inc=ch)
                    tai = reduce_last(eng, ch, b,
                                      b["att"][:, h4 * 4:(h4 + 1) * 4, :],
                                      b["prod"][:, :, :, :], (128, 4, 16, 64),
                                      Op.add, [tp])
                tmx = reduce_last(eng, ch, b, b["sm"][:, 0, :],
                                  b["att"][:, :, :], (128, 16, 16), Op.max, [tai])
                nms = b["sm"][:, 0, :].unsqueeze(2).broadcast_to((128, 16, 16))
                tsh = env.rec(eng, lambda e, nms=nms: e.tensor_tensor(
                    b["att"][:, :, :], b["att"][:, :, :], nms, op=Op.subtract),
                    [tmx], inc=ch)
                texp = env.rec("scalar", lambda e: e.activation(
                    out=b["eall"][:, :, :], in_=b["att"][:, :, :], func=AF.Exp,
                    scale=0.125), [tsh], inc="act")
                tden = reduce_last(eng, ch, b, b["sm"][:, 1, :],
                                   b["eall"][:, :, :], (128, 16, 16), Op.add,
                                   [texp])
                trd = env.rec("vector", lambda e: e.reciprocal(
                    b["sm"][:, 2, :], b["sm"][:, 1, :]), [tden], inc="dve")
                tout = None
                tv = b["tree"].rearrange("p (a g x) -> p a g x", g=8, x=64)
                for h4 in range(4):
                    es = b["eall"][:, h4 * 4:(h4 + 1) * 4, :].unsqueeze(3) \
                        .broadcast_to((128, 4, 16, 64))
                    vs = vpart.unsqueeze(1).broadcast_to((128, 4, 16, 64))
                    tp = env.rec(eng, lambda e, es=es, vs=vs: e.tensor_tensor(
                        b["prod"][:, :, :, :], es, vs, op=Op.mult),
                        [trd] if h4 == 0 else (), inc=ch)
                    env.rec(eng, lambda e, tv=tv: e.tensor_tensor(
                        tv[:, :, :, :], b["prod"][:, :, 0:8, :],
                        b["prod"][:, :, 8:16, :], op=Op.add), [tp])
                    env.rec(eng, lambda e, tv=tv: e.tensor_tensor(
                        tv[:, :, 0:4, :], tv[:, :, 0:4, :],
                        tv[:, :, 4:8, :], op=Op.add))
                    env.rec(eng, lambda e, tv=tv: e.tensor_tensor(
                        tv[:, :, 0:2, :], tv[:, :, 0:2, :],
                        tv[:, :, 2:4, :], op=Op.add))
                    tr4 = env.rec(eng, lambda e, tv=tv: e.tensor_tensor(
                        tv[:, :, 0, :], tv[:, :, 0, :],
                        tv[:, :, 1, :], op=Op.add), inc=ch)
                    ov = out_tok[:, j, :].rearrange("p (h x) -> p h x", x=64)
                    rds = b["sm"][:, 2, h4 * 4:(h4 + 1) * 4].unsqueeze(2) \
                        .broadcast_to((128, 4, 64))
                    tout = env.rec(eng,
                                   lambda e, tv=tv, h4=h4, ov=ov, rds=rds:
                                   e.tensor_tensor(
                                       ov[:, h4 * 4:(h4 + 1) * 4, :],
                                       tv[:, :, 0, :], rds, op=Op.mult),
                                   [tr4], inc=ch)
                return tout

            touts = [einsum_tile(j, "vector") for j in range(4)]
            t_qtok_free = touts[-1]

            t_otr_last = None
            t_ofm = None
            for m in range(NCH):
                bank = ptr[m % 2]
                tt = None
                for j in range(4):
                    tt = env.rec("tensor", lambda e, bank=bank, j=j, m=m: e.transpose(
                        bank.h[:, j, :], out_tok[:, j, m * 128:(m + 1) * 128],
                        env.identb[:, :]),
                        ([touts[0], touts[1], touts[2], touts[3], bank.tok]
                         if m < 2 else [bank.tok]) if j == 0 else (),
                        inc="pe" if j == 3 else None)
                t_otr_last = tt
                t_ofm = env.rec("vector", lambda e, bank=bank, m=m: e.tensor_copy(
                    out_fm[:, m, :],
                    bank.h[:, 0:4, :].rearrange("p a b -> p (a b)")),
                    [tt], inc="dve")
                bank.tok = t_ofm
            t_otok_free = t_otr_last

            ty2 = None
            for m in range(NCH):
                bank = pmm[m % 2]
                tmm = None
                for k in range(NCH):
                    tmm = env.rec("tensor", lambda e, bank=bank, k=k, m=m: e.matmul(
                        bank.h[:, :], wo_sb[:, k, m * 128:(m + 1) * 128],
                        out_fm[:, k, :], start=(k == 0), stop=(k == NCH - 1)),
                        ([t_ofm] if m == 0 else []) + [bank.tok] if k == 0 else (),
                        inc="pe" if k == NCH - 1 else None)
                ty2 = env.rec("vector",
                              lambda e, bank=bank, m=m: e.scalar_tensor_tensor(
                                  y[:, m, :], bank.h[:, :], vcol(env, "b_bo", m),
                                  y[:, m, :], op0=Op.add, op1=Op.add),
                              [tmm], inc="dve")
                bank.tok = ty2
            t_y_free = env.dma(
                "scalar",
                S1[n].rearrange("(c p) t -> p c t", p=128)[:, :, q0:q0 + SL],
                y[:, :, :], [ty2], ch="st")
            t_h_free = tmm
            t_h_free = t_kvmm
        env.flush("B1")


def emit_stageB2(env):
    nc = env.nc
    S0 = env.stream("S0")
    S1 = env.stream("S1")
    S2 = env.stream("S2")
    S2b = bstream(env, "S2")
    NTOK = NSEQ * T
    SL2 = 256
    with ExitStack() as st:
        sb = lambda nm, shp, dt=F32: st.enter_context(nc.sbuf_tensor(nm, shp, dt))
        ps = lambda nm, shp, dt=F32: st.enter_context(nc.psum_tensor(nm, shp, dt))
        we_sb = sb("b2_we", [128, 64, D], BF16)
        wg_sb = sb("b2_wg", [128, NCH, 8], BF16)
        y2s = sb("b2_y2", [128, NCH, SL2], F32)
        x1s = sb("b2_x1", [128, NCH, SL2], F32)
        cb = sb("b2_cb", [128, NCH, SL2], BF16)
        s2b = sb("b2_s2b", [128, 2, SL2], BF16)
        h2b = sb("b2_h2b", [128, NCH, SL2], BF16)
        h2s = sb("b2_h2s", [128, 8, NCH, SL2], BF16)
        gate_bc = sb("b2_gbc", [128, 8, SL2], F32)
        scf = sb("b2_scf", [128, SL2], F32)
        g_tok = sb("b2_gtok", [128, 2, 8], F32)
        gq = sb("b2_gq", [128, 2, 8], F32)
        gsc = sb("b2_gsc", [128, 8], F32)
        gmsk = sb("b2_gmsk", [128, 2, 3, 8], F32)
        grep = sb("b2_grep", [128, 128], F32)
        rows = sb("b2_rows", [1, 6, SL2], F32)
        nwt_i = sb("b2_nwi", [1, SL2], I32)
        pcomb = Bank(ps("b2_pcomb", [128, NCH, SL2], F32))
        prows = Bank(ps("b2_prows", [1, 2, SL2], F32))
        pbc = Bank(ps("b2_pbc", [128, SL2], F32))
        pmisc = Bank(ps("b2_pmisc", [128, SL2], F32))

        for e in range(8):
            for k in range(NCH):
                env.dma("sync", we_sb[:, e * 8 + k, :],
                        env.dram["bf_b_we"][e * 1024 + k * 128:
                                            e * 1024 + (k + 1) * 128, :],
                        [env.conv_tok.get("b_we")] if (e == 0 and k == 0) else ())
        wg_f = sb("b2_wgf", [128, NCH, 8], F32)
        for k in range(NCH):
            env.dma("sync", wg_f[:, k, :],
                    env.din("b_wg", [D, 8])[k * 128:(k + 1) * 128, :])
        t_wld = env.tok("ld")
        tcw = env.rec("vector", lambda e: e.tensor_copy(
            wg_sb[:, :, :], wg_f[:, :, :]), [t_wld], inc="dve")

        R_MEAN, R_VAR, R_RSTD, R_T1, R_SP = range(5)
        row = lambda r: rows[0:1, r, :]
        t_y2_free = []
        t_x1_free = None
        t_rows_free = None
        t_cb_free = None
        t_stb2 = None
        t_scf_free = None
        t_h2b_free = None
        t_h2s_free = None

        for s in range(NTOK // SL2):
            n, q0 = divmod(s * SL2, T)
            tld = env.dma("sync", y2s[:, :, :],
                          S1[n].rearrange("(c p) t -> p c t", p=128)[:, :, q0:q0 + SL2],
                          t_y2_free, ch="ld")
            tldx = env.dma("sync", x1s[:, :, :],
                           S0[n].rearrange("(c p) t -> p c t", p=128)[:, :, q0:q0 + SL2],
                           [t_x1_free], ch="ld")
            tsq = None
            for c in range(NCH):
                tsqd = env.rec("vector", lambda e, c=c: e.tensor_tensor(
                    s2b[:, c % 2, :], y2s[:, c, :], y2s[:, c, :], op=Op.mult),
                    ([tld] if c == 0 else []) + ([tsq] if c >= 2 else []),
                    inc="dve")
                tsq = env.rec("tensor", lambda e, c=c: e.matmul(
                    prows.h[:1, 0, :], env.onescol[:, :], s2b[:, c % 2, :],
                    start=(c == 0), stop=(c == NCH - 1)),
                    [tsqd] + ([prows.tok, tcw if s == 0 else None]
                              if c == 0 else []), inc="pe")
            env.rec("vector", lambda e: e.tensor_scalar(
                row(R_VAR), prows.h[0:1, 0, :], 1.0 / D, RMS_EPS,
                op0=Op.mult, op1=Op.add), [tsq, t_rows_free])
            trst = emit_rsqrt(env, row(R_VAR), row(R_RSTD), nwt_i[0:1, :],
                              row(R_T1), inc="dve")
            prows.tok = trst
            tb_r = _bcast(env, row(R_RSTD), pbc.h[:, :], [trst, pbc.tok])
            t_rows_free = tb_r
            th2 = None
            for c in range(NCH):
                th2 = env.rec("vector", lambda e, c=c: e.scalar_tensor_tensor(
                    h2b[:, c, :], y2s[:, c, :], vcol(env, "b_g2", c), pbc.h[:, :],
                    op0=Op.mult, op1=Op.mult),
                    [tb_r, t_h2b_free] if c == 0 else (),
                    inc="dve" if c == NCH - 1 else None)
            pbc.tok = th2
            tmm = None
            for k in range(NCH):
                tmm = env.rec("tensor", lambda e, k=k: e.matmul(
                    pmisc.h[0:8, :], wg_sb[:, k, :], h2b[:, k, :],
                    start=(k == 0), stop=(k == NCH - 1)),
                    [th2, pmisc.tok] if k == 0 else (),
                    inc="pe" if k == NCH - 1 else None)
            t_h2b_rd0 = tmm
            tsc = env.rec("scalar", lambda e: e.activation(
                out=scf[0:8, :], in_=pmisc.h[0:8, :], func=AF.Identity,
                bias=env.vtab[0:8, VIDX["b_bg"]:VIDX["b_bg"] + 1]),
                [tmm, t_scf_free], inc="act")
            pmisc.tok = tsc
            tts = None
            for j in range(SL2 // 128):
                tts = env.rec("tensor", lambda e, j=j: e.transpose(
                    pmisc.h[:, j * 128:(j + 1) * 128], scf[:, j * 128:(j + 1) * 128],
                    env.identf[:, :]),
                    [tsc, pmisc.tok] if j == 0 else (),
                    inc="pe" if j == SL2 // 128 - 1 else None)
            t_scf_free = tts
            tgt = env.rec("vector", lambda e: e.tensor_copy(
                g_tok[:, :, :],
                pmisc.h[:, :].rearrange("p (j x) -> p j x", x=128)[:, :, 0:8]),
                [tts], inc="dve")
            pmisc.tok = tgt
            tgq = tgt
            for j in range(SL2 // 128):
                gj = g_tok[:, j, :]
                ge = gmsk[:, j, 0, :]
                seq = [
                    ("vector", lambda e, gj=gj: e.reduce_max(
                        gsc[:, 0:1], gj, axis=AX, negate=True)),
                    ("scalar", lambda e, gj=gj, ge=ge: e.activation(
                        out=ge, in_=gj, func=AF.Exp, bias=gsc[:, 0:1],
                        accum_out=gsc[:, 1:2])),
                    ("vector", lambda e: e.reciprocal(gsc[:, 2:3], gsc[:, 1:2])),
                    ("vector", lambda e, ge=ge, j=j: e.tensor_scalar(
                        gq[:, j, :], ge, gsc[:, 2:3], None, op0=Op.mult)),
                ]
                for eng, fn in seq:
                    tgq = env.rec(eng, fn, [tgq],
                                  inc="dve" if eng == "vector" else "act")
            for e_ in range(8):
                ttj = None
                for j in range(SL2 // 128):
                    trp = env.rec("vector", lambda e, e_=e_, j=j: e.tensor_copy(
                        grep[:, :], gq[:, j, e_:e_ + 1].broadcast_to((128, 128))),
                        ([tgq] if (e_ == 0 and j == 0) else []) +
                        ([pmisc.tok] if j == 0 else []) +
                        ([ttj] if j > 0 else []), inc="dve")
                    ttj = env.rec("tensor", lambda e, j=j: e.transpose(
                        pmisc.h[:, j * 128:(j + 1) * 128], grep[:, :],
                        env.identf[:, :]), [trp], inc="pe")
                pmisc.tok = env.rec("vector", lambda e, e_=e_: e.tensor_copy(
                    gate_bc[:, e_, :], pmisc.h[:, :]), [ttj], inc="dve")
            t_gbc = pmisc.tok
            if s == 0:
                env.tap("b2_h2b", h2b[:, :, :], [th2])
                env.tap("b2_scf", scf[:, :], [tsc])
                env.tap("b2_gtok", g_tok[:, :, :], [tgt])
                env.tap("b2_gq", gq[:, :, :], [tgq])
                env.tap("b2_gbc", gate_bc[:, :, :], [t_gbc])

            ths = None
            for e_ in range(8):
                for c in range(NCH):
                    ths = env.rec("vector", lambda e, e_=e_, c=c: e.tensor_tensor(
                        h2s[:, e_, c, :], h2b[:, c, :], gate_bc[:, e_, :],
                        op=Op.mult),
                        [t_gbc, t_h2s_free] if (e_ == 0 and c == 0) else (),
                        inc="dve" if (e_ == 7 and c == NCH - 1) else None)
            tlast_mm = None
            for m in range(NCH):
                for e_ in range(8):
                    for k in range(NCH):
                        tlast_mm = env.rec(
                            "tensor", lambda e, e_=e_, m=m, k=k: e.matmul(
                                pcomb.h[:, m, :],
                                we_sb[:, e_ * 8 + k, m * 128:(m + 1) * 128],
                                h2s[:, e_, k, :],
                                start=(e_ == 0 and k == 0),
                                stop=(e_ == 7 and k == NCH - 1)),
                            [ths, pcomb.tok] if (m == 0 and e_ == 0 and k == 0)
                            else (),
                            inc="pe" if (e_ == 7 and k == NCH - 1) else None)
            t_h2s_free = tlast_mm
            tx2 = None
            for m in range(NCH):
                env.rec("vector", lambda e, m=m: e.tensor_tensor(
                    y2s[:, m, :], y2s[:, m, :], pcomb.h[:, m, :], op=Op.add),
                    [tlast_mm] if m == 0 else ())
                tx2 = env.rec("vector", lambda e, m=m: e.tensor_tensor(
                    y2s[:, m, :], y2s[:, m, :], x1s[:, m, :], op=Op.add),
                    [tldx] if m == 0 else (), inc="dve" if m == NCH - 1 else None)
            pcomb.tok = tx2
            t_st1 = env.dma(
                "scalar",
                S1[n].rearrange("(c p) t -> p c t", p=128)[:, :, q0:q0 + SL2],
                y2s[:, :, :], [tx2], ch="st")
            tcb = None
            for c in range(NCH):
                tcb = env.rec("vector", lambda e, c=c: e.tensor_copy(
                    cb[:, c, :], y2s[:, c, :]),
                    [tx2, t_cb_free] if c == 0 else (),
                    inc="dve" if c == NCH - 1 else None)
            tsum = None
            for c in range(NCH):
                tsum = env.rec("tensor", lambda e, c=c: e.matmul(
                    prows.h[:1, 0, :], env.onescol[:, :], cb[:, c, :],
                    start=(c == 0), stop=(c == NCH - 1)),
                    [tcb, prows.tok] if c == 0 else (),
                    inc="pe" if c == NCH - 1 else None)
            tsq2 = None
            for c in range(NCH):
                tsqd2 = env.rec("vector", lambda e, c=c: e.tensor_tensor(
                    s2b[:, c % 2, :], cb[:, c, :], cb[:, c, :], op=Op.mult),
                    [tsq2] if c >= 2 else (), inc="dve")
                tsq2 = env.rec("tensor", lambda e, c=c: e.matmul(
                    prows.h[:1, 1, :], env.onescol[:, :], s2b[:, c % 2, :],
                    start=(c == 0), stop=(c == NCH - 1)), [tsqd2], inc="pe")
            t_cb_free = tsq2
            env.rec("vector", lambda e: e.tensor_scalar(
                row(R_MEAN), prows.h[0:1, 0, :], 1.0 / D, None, op0=Op.mult),
                [tsum, tsq2, t_rows_free])
            env.rec("vector", lambda e: e.tensor_tensor(
                row(R_T1), row(R_MEAN), row(R_MEAN), op=Op.mult))
            env.rec("vector", lambda e: e.tensor_scalar(
                row(R_VAR), prows.h[0:1, 1, :], 1.0 / D, LN_EPS,
                op0=Op.mult, op1=Op.add))
            tvar = env.rec("vector", lambda e: e.tensor_tensor(
                row(R_VAR), row(R_VAR), row(R_T1), op=Op.subtract), inc="dve")
            prows.tok = tvar
            trst2 = emit_rsqrt(env, row(R_VAR), row(R_RSTD), nwt_i[0:1, :],
                               row(R_T1), inc="dve")
            tb_m = _bcast(env, row(R_MEAN), pbc.h[:, :], [trst2, pbc.tok])
            tsb = None
            for c in range(NCH):
                tsb = env.rec("vector", lambda e, c=c: e.tensor_tensor(
                    x1s[:, c, :], y2s[:, c, :], pbc.h[:, :], op=Op.subtract),
                    [tb_m] if c == 0 else (), inc="dve" if c == NCH - 1 else None)
            pbc.tok = tsb
            tb_r2 = _bcast(env, row(R_RSTD), pbc.h[:, :], [tsb, pbc.tok])
            t_rows_free = tb_r2
            txs = None
            for c in range(NCH):
                txs = env.rec("vector", lambda e, c=c: e.tensor_tensor(
                    x1s[:, c, :], x1s[:, c, :], pbc.h[:, :], op=Op.mult),
                    [tb_r2] if c == 0 else (), inc="dve" if c == NCH - 1 else None)
            pbc.tok = txs
            t_st2 = env.dma(
                "scalar",
                S2[n].rearrange("(c p) t -> p c t", p=128)[:, :, q0:q0 + SL2],
                x1s[:, :, :], [txs], ch="st")
            tcb2 = env.rec("vector", lambda e: e.tensor_copy(
                cb[:, :, :].rearrange("p c n -> p (c n)"),
                x1s[:, :, :].rearrange("p c n -> p (c n)")),
                [txs, t_stb2], inc="dve")
            t_stb2 = env.dma(
                "scalar",
                S2b[n].rearrange("(c p) t -> p c t", p=128)[:, :, q0:q0 + SL2],
                cb[:, :, :], [tcb2], ch="st")
            t_y2_free = [t_st1, txs]
            t_x1_free = t_st2
            t_h2b_free = tlast_mm
            t_cb_free = tcb2
        env.flush("B2")



def bstream(env, name):
    kind = env.stream_kinds.get(name + "_bf")
    if kind == "in":
        return env.din(name + "_bf", [NSEQ, D, T], BF16)
    return env.dtmp(name + "_bf", [NSEQ, D, T], BF16)


def bview(x3d):
    return x3d.rearrange("p c n -> p (c n)")


def bb(ap2d, n):
    return ap2d.unsqueeze(2).broadcast_to((ap2d.shape[0], ap2d.shape[1], n))


def emit_ln_batched(env, u, castb, rows, nwt_i, prow_s, prow_q, eps,
                    t_u, t_cast_free, t_rows_free):
    row = lambda r: rows[0:1, r, :]
    tc = env.rec("vector", lambda e: e.tensor_copy(bview(castb), bview(u)),
                 [t_u, t_cast_free], inc="dve")
    tsum = None
    for c in range(NCH):
        tsum = env.rec("tensor", lambda e, c=c: e.matmul(
            prow_s.h[:1, :], env.onescol[:, :], castb[:, c, :],
            start=(c == 0), stop=(c == NCH - 1)),
            [tc, prow_s.tok] if c == 0 else (),
            inc="pe" if c == NCH - 1 else None)
    tsq0 = env.rec("vector", lambda e: e.tensor_tensor(
        bview(castb), bview(u), bview(u), op=Op.mult),
        [tsum], inc="dve")
    tsq = None
    for c in range(NCH):
        tsq = env.rec("tensor", lambda e, c=c: e.matmul(
            prow_q.h[:1, :], env.onescol[:, :], castb[:, c, :],
            start=(c == 0), stop=(c == NCH - 1)),
            [tsq0, prow_q.tok] if c == 0 else (),
            inc="pe" if c == NCH - 1 else None)
    env.rec("vector", lambda e: e.tensor_scalar(
        row(0), prow_s.h[:1, :], 1.0 / D, None, op0=Op.mult),
        [tsum, tsq, t_rows_free])
    env.rec("vector", lambda e: e.tensor_tensor(
        row(3), row(0), row(0), op=Op.mult))
    env.rec("vector", lambda e, eps=eps: e.tensor_scalar(
        row(1), prow_q.h[:1, :], 1.0 / D, eps, op0=Op.mult, op1=Op.add))
    tvar = env.rec("vector", lambda e: e.tensor_tensor(
        row(1), row(1), row(3), op=Op.subtract), inc="dve")
    prow_s.tok = tvar
    prow_q.tok = tvar
    trst = emit_rsqrt(env, row(1), row(2), nwt_i[0:1, :], row(3), inc="dve")
    return trst, tsq


def emit_ln_apply_batched(env, u, dst, rows, pbc, bcb, scols, bcols, trst,
                          t_dst_free):
    row = lambda r: rows[0:1, r, :]
    N = u.shape[2]
    tb_m = _bcast(env, row(0), pbc[0].h[:, :], [trst, pbc[0].tok])
    tb_r = _bcast(env, row(2), pbc[1].h[:, :], [trst, pbc[1].tok])
    tcm = env.rec("vector", lambda e: e.tensor_copy(
        bcb[:, 0, :], pbc[0].h[:, :]), [tb_m], inc="dve")
    pbc[0].tok = tcm
    tcr = env.rec("vector", lambda e: e.tensor_copy(
        bcb[:, 1, :], pbc[1].h[:, :]), [tb_r], inc="dve")
    pbc[1].tok = tcr
    mbb = bcb[:, 0, :].unsqueeze(1).broadcast_to((128, NCH, N))
    rbb = bcb[:, 1, :].unsqueeze(1).broadcast_to((128, NCH, N))
    tsb = env.rec("vector", lambda e: e.tensor_tensor(
        u, u, mbb, op=Op.subtract), [tcm])
    if scols is None:
        tap = env.rec("vector", lambda e: e.tensor_tensor(
            dst, u, rbb, op=Op.mult), [tcr, t_dst_free], inc="dve")
    else:
        env.rec("vector", lambda e: e.tensor_tensor(
            u, u, rbb, op=Op.mult), [tcr])
        env.rec("vector", lambda e: e.tensor_tensor(
            dst, u, bb(scols, N), op=Op.mult), [t_dst_free])
        tap = env.rec("vector", lambda e: e.tensor_tensor(
            dst, dst, bb(bcols, N), op=Op.add), inc="dve")
    return tap, tap


def vcols(env, name, n=NCH):
    r = VIDX[name]
    return env.vtab[:, r:r + n]


def emit_stageC(env, blocks=tuple(range(8)), s_in="S2", ping=("S0", "S2"),
                s_tmp="S3", dev_out=None):
    nc = env.nc
    NTOK = NSEQ * T
    NSLC = NTOK // SL
    blocks = list(blocks)

    Stmp = env.stream(s_tmp)
    Stmpb = bstream(env, s_tmp)

    with ExitStack() as stC:
        sbc = lambda nm, shp, dt=F32: stC.enter_context(nc.sbuf_tensor(nm, shp, dt))
        kvfin = sbc("c_kvfin", [128, NCH, NSEQ], F32)
        ksfin = sbc("c_ksfin", [128, NCH, NSEQ], F32)
        kvp = sbc("c_kvp", [128, NCH, NSL], F32)
        ksp = sbc("c_ksp", [128, NCH, NSL], F32)

        for i in blocks:
            S_in = env.stream(s_in) if i == blocks[0] else \
                env.stream(ping[(i - 1) % 2])
            S_in_b = bstream(env, s_in) if i == blocks[0] else \
                bstream(env, ping[(i - 1) % 2])
            if dev_out is not None and i == blocks[-1]:
                S_out = env.stream(dev_out)
                S_out_b = bstream(env, dev_out)
            else:
                S_out = env.stream(ping[i % 2])
                S_out_b = bstream(env, ping[i % 2])
            bi = lambda nm: f"c_{nm}{i}"

            with ExitStack() as st:
                sb = lambda nm, shp, dt=F32: st.enter_context(
                    nc.sbuf_tensor(bi(nm), shp, dt))
                ps = lambda nm, shp, dt=F32: st.enter_context(
                    nc.psum_tensor(bi(nm), shp, dt))
                wk_sb = sb("wk", [128, NCH, D], BF16)
                wv_sb = sb("wv", [128, NCH, D], BF16)
                xb = sb("p1xb", [128, NCH, SL], BF16)
                kraw = sb("p1kr", [128, NCH, SL], F32)
                ta = sb("p1ta", [128, NCH, SL], F32)
                Kf = sb("p1K", [128, NCH, SL], F32)
                vf = sb("p1v", [128, NCH, SL], F32)
                pk = [Bank(ps(f"p1pk{b}", [128, SL], F32)) for b in range(2)]
                pv = [Bank(ps(f"p1pv{b}", [128, SL], F32)) for b in range(2)]
                for k in range(NCH):
                    env.dma("sync", wk_sb[:, k, :],
                            env.dram["bf_c_wk"][(i * 8 + k) * 128:
                                                (i * 8 + k + 1) * 128, :],
                            [env.conv_tok.get("c_wk")] if k == 0 else ())
                    env.dma("sync", wv_sb[:, k, :],
                            env.dram["bf_c_wv"][(i * 8 + k) * 128:
                                                (i * 8 + k + 1) * 128, :],
                            [env.conv_tok.get("c_wv")] if k == 0 else ())
                t_w = env.tok("ld")
                t_xb_free = None
                t_kraw_free = None
                t_vf_free = None
                t_Kf_free = None
                t_acc = None
                for s in range(NSLC):
                    n, q0 = divmod(s * SL, T)
                    sw = s % NSL
                    tld = env.dma("sync", xb[:, :, :],
                                  S_in_b[n].rearrange("(c p) t -> p c t", p=128)
                                  [:, :, q0:q0 + SL], [t_xb_free], ch="ld")
                    tkev = None
                    tlast = None
                    for m in range(NCH):
                        bank = pk[m % 2]
                        tmm = None
                        for k in range(NCH):
                            tmm = env.rec(
                                "tensor", lambda e, bank=bank, k=k, m=m: e.matmul(
                                    bank.h[:, :],
                                    wk_sb[:, k, m * 128:(m + 1) * 128],
                                    xb[:, k, :], start=(k == 0),
                                    stop=(k == NCH - 1)),
                                ([tld, t_w if s == 0 else None]
                                 if m == 0 else []) + [bank.tok]
                                if k == 0 else (),
                                inc="pe" if k == NCH - 1 else None)
                        tkev = env.rec("scalar", lambda e, bank=bank, m=m:
                                       e.activation(out=kraw[:, m, :],
                                                    in_=bank.h[:, :],
                                                    func=AF.Identity,
                                                    bias=vcol(env, f"c_bk{i}", m)),
                                       [tmm] + ([t_kraw_free] if m == 0 else []),
                                       inc="act")
                        bank.tok = tkev
                    t1 = env.rec("vector", lambda e: e.tensor_scalar(
                        bview(ta), bview(kraw), 0.0, None, op0=Op.min),
                        [tkev, t_acc], inc="dve")
                    t2 = env.rec("scalar", lambda e: e.activation(
                        out=bview(Kf), in_=bview(ta), func=AF.Exp),
                        [t1, t_Kf_free], inc="act")
                    t3 = env.rec("vector", lambda e: e.tensor_scalar(
                        bview(kraw), bview(kraw), 0.0, None, op0=Op.max),
                        [t2], inc="dve")
                    tK = env.rec("vector", lambda e: e.tensor_tensor(
                        bview(Kf), bview(Kf), bview(kraw), op=Op.add),
                        [t2, t3], inc="dve")
                    tvev = None
                    for m in range(NCH):
                        bank = pv[m % 2]
                        tmm = None
                        for k in range(NCH):
                            tmm = env.rec(
                                "tensor", lambda e, bank=bank, k=k, m=m: e.matmul(
                                    bank.h[:, :],
                                    wv_sb[:, k, m * 128:(m + 1) * 128],
                                    xb[:, k, :], start=(k == 0),
                                    stop=(k == NCH - 1)),
                                [bank.tok] if k == 0 else (),
                                inc="pe" if k == NCH - 1 else None)
                        if m == NCH - 1:
                            t_xb_free = tmm
                        tvev = env.rec("scalar", lambda e, bank=bank, m=m:
                                       e.activation(out=vf[:, m, :],
                                                    in_=bank.h[:, :],
                                                    func=AF.Identity,
                                                    bias=vcol(env, f"c_bv{i}", m)),
                                       [tmm] + ([t_vf_free] if m == 0 else []),
                                       inc="act")
                        bank.tok = tvev
                    tpr = env.rec("vector", lambda e: e.tensor_tensor(
                        bview(kraw), bview(Kf), bview(vf), op=Op.mult),
                        [tK, tvev], inc="dve")
                    t_kraw_free = tpr
                    t_vf_free = tpr
                    env.rec("vector", lambda e, sw=sw: e.reduce_sum(
                        kvp[:, :, sw], kraw[:, :, :], axis=AX), [tpr])
                    t_acc = env.rec("vector", lambda e, sw=sw: e.reduce_sum(
                        ksp[:, :, sw], Kf[:, :, :], axis=AX), inc="dve")
                    t_Kf_free = t_acc
                    if sw == NSL - 1:
                        t_acc = env.rec("vector", lambda e, n=n: e.reduce_sum(
                            kvfin[:, :, n], kvp[:, :, :], axis=AX),
                            [t_acc], inc="dve")
                        t_acc = env.rec("vector", lambda e, n=n: e.reduce_sum(
                            ksfin[:, :, n], ksp[:, :, :], axis=AX),
                            [t_acc], inc="dve")
                if i == blocks[0]:
                    env.tap("kvfin", kvfin[:, :, :], [t_acc])
                    env.tap("ksfin", ksfin[:, :, :], [t_acc])
                env.flush(f"C{i}_P1")

            with ExitStack() as st:
                sb = lambda nm, shp, dt=F32: st.enter_context(
                    nc.sbuf_tensor(bi(nm), shp, dt))
                ps = lambda nm, shp, dt=F32: st.enter_context(
                    nc.psum_tensor(bi(nm), shp, dt))
                wq_sb = sb("wq", [128, NCH, D], BF16)
                wo_sb = sb("wo", [128, NCH, D], BF16)
                xb = sb("axb", [128, NCH, SL], BF16)
                xs = sb("ax", [128, NCH, SL], F32)
                qraw = sb("aqr", [128, NCH, SL], F32)
                t1b = sb("at1", [128, NCH, SL], F32)
                Qf = sb("aQ", [128, NCH, SL], F32)
                Vb = sb("aV", [128, NCH, SL], BF16)
                xat = sb("axat", [128, NCH, SL], F32)
                u1 = sb("au1", [128, NCH, SL], F32)
                xpb = sb("axpb", [128, NCH, SL], BF16)
                bcb = sb("abcb", [128, 2, SL], F32)
                rows = sb("arows", [1, 4, SL], F32)
                nwt_i = sb("anwi", [1, SL], I32)
                pq = [Bank(ps(f"apq{b}", [128, SL], F32)) for b in range(2)]
                pbc = [Bank(ps(f"apbc{b}", [128, SL], F32)) for b in range(2)]
                prow_s = Bank(ps("aprs", [1, SL], F32))
                prow_q = Bank(ps("aprq", [1, SL], F32))
                for k in range(NCH):
                    env.dma("sync", wq_sb[:, k, :],
                            env.dram["bf_c_wq"][(i * 8 + k) * 128:
                                                (i * 8 + k + 1) * 128, :],
                            [env.conv_tok.get("c_wq")] if k == 0 else ())
                    env.dma("sync", wo_sb[:, k, :],
                            env.dram["bf_c_wo"][(i * 8 + k) * 128:
                                                (i * 8 + k + 1) * 128, :],
                            [env.conv_tok.get("c_wo")] if k == 0 else ())
                t_w = env.tok("ld")
                t_xb_free = None
                t_x_free = None
                t_qraw_free = None
                t_t1b_free = None
                t_Qf_free = None
                t_Vb_free = None
                t_xat_free = None
                t_u1_free = None
                t_rows_free = None
                t_st = None
                t_stb = None
                ksbb = {n: bb(ksfin[:, :, n], SL) for n in range(NSEQ)}
                kvbb = {n: bb(kvfin[:, :, n], SL) for n in range(NSEQ)}
                g1s = GELU_S * GELU_C
                for s in range(NSLC):
                    n, q0 = divmod(s * SL, T)
                    rr = lambda S: S[n].rearrange("(c p) t -> p c t", p=128)[
                        :, :, q0:q0 + SL]
                    tld = env.dma("sync", xb[:, :, :], rr(S_in_b),
                                  [t_xb_free], ch="ld")
                    tldx = env.dma("sync", xs[:, :, :], rr(S_in),
                                   [t_x_free], ch="ld2")
                    tqev = None
                    for m in range(NCH):
                        bank = pq[m % 2]
                        tmm = None
                        for k in range(NCH):
                            tmm = env.rec(
                                "tensor", lambda e, bank=bank, k=k, m=m: e.matmul(
                                    bank.h[:, :],
                                    wq_sb[:, k, m * 128:(m + 1) * 128],
                                    xb[:, k, :], start=(k == 0),
                                    stop=(k == NCH - 1)),
                                ([tld, t_w if s == 0 else None]
                                 if m == 0 else []) + [bank.tok]
                                if k == 0 else (),
                                inc="pe" if k == NCH - 1 else None)
                        tqev = env.rec("scalar", lambda e, bank=bank, m=m:
                                       e.activation(out=qraw[:, m, :],
                                                    in_=bank.h[:, :],
                                                    func=AF.Identity,
                                                    bias=vcol(env, f"c_bq{i}", m)),
                                       [tmm] + ([t_qraw_free] if m == 0 else []),
                                       inc="act")
                        bank.tok = tqev
                    t1 = env.rec("vector", lambda e: e.tensor_scalar(
                        bview(t1b), bview(qraw), 0.0, None, op0=Op.min),
                        [tqev, t_t1b_free], inc="dve")
                    t2 = env.rec("scalar", lambda e: e.activation(
                        out=bview(Qf), in_=bview(t1b), func=AF.Exp),
                        [t1, t_Qf_free], inc="act")
                    t3 = env.rec("vector", lambda e: e.tensor_scalar(
                        bview(qraw), bview(qraw), 0.0, None, op0=Op.max), [t2])
                    tQ = env.rec("vector", lambda e: e.tensor_tensor(
                        bview(Qf), bview(Qf), bview(qraw), op=Op.add),
                        [t2], inc="dve")
                    tz = env.rec("vector", lambda e, n=n: e.tensor_tensor(
                        t1b[:, :, :], Qf[:, :, :], ksbb[n], op=Op.mult), [tQ])
                    tz = env.rec("vector", lambda e: e.tensor_scalar(
                        bview(t1b), bview(t1b), 1.0, C_Z_EPS,
                        op0=Op.mult, op1=Op.add))
                    tz = env.rec("vector", lambda e: e.reciprocal(
                        bview(qraw), bview(t1b)), inc="dve")
                    tz = env.rec("vector", lambda e: e.tensor_tensor(
                        bview(qraw), bview(Qf), bview(qraw), op=Op.mult), [tz])
                    tVb = env.rec("vector", lambda e, n=n: e.tensor_tensor(
                        Vb[:, :, :], qraw[:, :, :], kvbb[n], op=Op.mult),
                        [t_Vb_free], inc="dve")
                    txev = None
                    for m in range(NCH):
                        bank = pq[m % 2]
                        tmm = None
                        for k in range(NCH):
                            tmm = env.rec(
                                "tensor", lambda e, bank=bank, k=k, m=m: e.matmul(
                                    bank.h[:, :],
                                    wo_sb[:, k, m * 128:(m + 1) * 128],
                                    Vb[:, k, :], start=(k == 0),
                                    stop=(k == NCH - 1)),
                                ([tVb] if m == 0 else []) + [bank.tok]
                                if k == 0 else (),
                                inc="pe" if k == NCH - 1 else None)
                        if m == NCH - 1:
                            t_Vb_free = tmm
                        txev = env.rec("scalar", lambda e, bank=bank, m=m:
                                       e.activation(out=xat[:, m, :],
                                                    in_=bank.h[:, :],
                                                    func=AF.Identity,
                                                    bias=vcol(env, f"c_bo{i}", m)),
                                       [tmm] + ([t_xat_free] if m == 0 else []),
                                       inc="act")
                        bank.tok = txev
                    tg = env.rec("scalar", lambda e: e.activation(
                        out=bview(t1b), in_=bview(xat), func=AF.Square),
                        [txev, tVb], inc="act")
                    tg = env.rec("vector", lambda e, g1s=g1s: e.tensor_scalar(
                        bview(t1b), bview(t1b), g1s, GELU_S,
                        op0=Op.mult, op1=Op.add), [tg], inc="dve")
                    tg = env.rec("vector", lambda e: e.tensor_tensor(
                        bview(t1b), bview(t1b), bview(xat), op=Op.mult),
                        inc="dve")
                    tg = env.rec("scalar", lambda e: e.activation(
                        out=bview(qraw), in_=bview(t1b), func=AF.Tanh),
                        [tg], inc="act")
                    tg = env.rec("vector", lambda e: e.scalar_tensor_tensor(
                        bview(t1b), bview(qraw), 1.0, bview(xat),
                        op0=Op.add, op1=Op.mult), [tg], inc="dve")
                    t_t1b_free = tg
                    t_qraw_free = tg
                    tu1 = env.rec("vector", lambda e: e.scalar_tensor_tensor(
                        bview(u1), bview(t1b), 0.5, bview(xs),
                        op0=Op.mult, op1=Op.add),
                        [tg, tldx, t_u1_free], inc="dve")
                    t_x_free = tu1
                    t_xat_free = tu1
                    t_Qf_free = tu1
                    trst, tcmm = emit_ln_batched(
                        env, u1[:, :, :], Vb, rows, nwt_i, prow_s, prow_q,
                        C_LN_EPS, tu1, t_Vb_free, t_rows_free)
                    t_Vb_free = tcmm
                    tap, t_rows_free = emit_ln_apply_batched(
                        env, u1[:, :, :], Qf[:, :, :], rows, pbc, bcb,
                        vcols(env, f"c_l1s{i}"), vcols(env, f"c_l1b{i}"),
                        trst, t_st)
                    t_u1_free = tap
                    t_Qf_free = tap
                    if i == blocks[0] and s == 0:
                        env.tap("aVb", Vb[:, :, :], [tVb])
                        env.tap("axat", xat[:, :, :], [txev])
                        env.tap("vtab", env.vtab[:, 0:256], [tap])
                        env.tap("acore", u1[:, :, :], [tap])
                        env.tap("axs", xs[:, :, :], [tu1])
                        env.tap("at1b", t1b[:, :, :], [tu1])
                    tcb = env.rec("vector", lambda e: e.tensor_copy(
                        bview(xpb), bview(Qf)), [tap, t_stb], inc="dve")
                    t_st = env.dma("scalar", rr(Stmp), Qf[:, :, :], [tap],
                                   ch="st")
                    t_stb = env.dma("scalar", rr(Stmpb), xpb[:, :, :], [tcb],
                                    ch="st")
                    t_Qf_free = t_st
                env.flush(f"C{i}_P2a")

            with ExitStack() as st:
                sb = lambda nm, shp, dt=F32: st.enter_context(
                    nc.sbuf_tensor(bi(nm), shp, dt))
                ps = lambda nm, shp, dt=F32: st.enter_context(
                    nc.psum_tensor(bi(nm), shp, dt))
                wf1_sb = sb("wf1", [128, NCH, 4096], BF16)
                wf2_sb = sb("wf2", [128, 32, D], BF16)
                xb = sb("bxb", [128, NCH, SL], BF16)
                xs = sb("bx", [128, NCH, SL], F32)
                fb = sb("bf", [128, 32, SL], BF16)
                tmp = sb("btmp", [128, SL], F32)
                bcb = sb("bbcb", [128, 2, SL], F32)
                rows = sb("brows", [1, 4, SL], F32)
                nwt_i = sb("bnwi", [1, SL], I32)
                pf = [Bank(ps(f"bpf{b}", [128, SL], F32)) for b in range(2)]
                pbc = [Bank(ps(f"bpbc{b}", [128, SL], F32)) for b in range(2)]
                prow_s = Bank(ps("bprs", [1, SL], F32))
                prow_q = Bank(ps("bprq", [1, SL], F32))
                obf = fb[:, 0:8, :]
                for k in range(NCH):
                    env.dma("sync", wf1_sb[:, k, :],
                            env.dram["bf_c_wf1"][(i * 8 + k) * 128:
                                                 (i * 8 + k + 1) * 128, :],
                            [env.conv_tok.get("c_wf1")] if k == 0 else ())
                for k in range(32):
                    env.dma("sync", wf2_sb[:, k, :],
                            env.dram["bf_c_wf2"][(i * 32 + k) * 128:
                                                 (i * 32 + k + 1) * 128, :],
                            [env.conv_tok.get("c_wf2")] if k == 0 else ())
                t_w = env.tok("ld")
                t_xb_free = None
                t_x_free = None
                t_fb_free = None
                t_rows_free = None
                t_st = None
                t_stb = None
                for s in range(NSLC):
                    n, q0 = divmod(s * SL, T)
                    rr = lambda S: S[n].rearrange("(c p) t -> p c t", p=128)[
                        :, :, q0:q0 + SL]
                    tld = env.dma("sync", xb[:, :, :], rr(Stmpb),
                                  [t_xb_free], ch="ld")
                    tldx = env.dma("sync", xs[:, :, :], rr(Stmp),
                                   [t_x_free], ch="ld2")
                    tf = None
                    for fc in range(32):
                        bank = pf[fc % 2]
                        tmm = None
                        for k in range(NCH):
                            tmm = env.rec(
                                "tensor", lambda e, bank=bank, k=k, fc=fc: e.matmul(
                                    bank.h[:, :],
                                    wf1_sb[:, k, fc * 128:(fc + 1) * 128],
                                    xb[:, k, :], start=(k == 0),
                                    stop=(k == NCH - 1)),
                                ([tld, t_w if s == 0 else None, t_fb_free]
                                 if fc == 0 else []) + [bank.tok]
                                if k == 0 else (),
                                inc="pe" if k == NCH - 1 else None)
                        if fc == 31:
                            t_xb_free = tmm
                        tf = env.rec("scalar", lambda e, bank=bank, fc=fc:
                                     e.activation(out=fb[:, fc, :],
                                                  in_=bank.h[:, :],
                                                  func=AF.Gelu_apprx_tanh,
                                                  bias=vcol(env, f"c_bf1_{i}", fc)),
                                     [tmm] + ([t_stb] if fc == 0 else []),
                                     inc="act")
                        bank.tok = tf
                    tu2 = None
                    t_fb_free = None
                    for m in range(NCH):
                        bank = pf[m % 2]
                        tmm = None
                        for k in range(32):
                            tmm = env.rec(
                                "tensor", lambda e, bank=bank, k=k, m=m: e.matmul(
                                    bank.h[:, :],
                                    wf2_sb[:, k, m * 128:(m + 1) * 128],
                                    fb[:, k, :], start=(k == 0),
                                    stop=(k == 31)),
                                ([tf] if m == 0 else []) + [bank.tok]
                                if k == 0 else (),
                                inc="pe" if k == 31 else None)
                        if m == NCH - 1:
                            t_fb_free = tmm
                        tgl = env.rec("scalar", lambda e, bank=bank, m=m:
                                      e.activation(out=tmp[:, :],
                                                   in_=bank.h[:, :],
                                                   func=AF.Gelu_apprx_tanh,
                                                   bias=vcol(env, f"c_bf2_{i}", m)),
                                      [tmm] + ([tu2] if m > 0 else [tldx]),
                                      inc="act")
                        tu2 = env.rec("vector", lambda e, m=m: e.tensor_tensor(
                            xs[:, m, :], xs[:, m, :], tmp[:, :], op=Op.add),
                            [tgl], inc="dve")
                        bank.tok = tu2
                    trst, tcmm = emit_ln_batched(
                        env, xs[:, :, :], xb, rows, nwt_i, prow_s, prow_q,
                        C_LN_EPS, tu2, t_xb_free, t_rows_free)
                    t_xb_free = tcmm
                    tap, t_rows_free = emit_ln_apply_batched(
                        env, xs[:, :, :], xs[:, :, :], rows, pbc, bcb,
                        vcols(env, f"c_l2s{i}"), vcols(env, f"c_l2b{i}"),
                        trst, t_st)
                    tcb = env.rec("vector", lambda e: e.tensor_copy(
                        bview(obf), bview(xs)), [tap, t_stb, t_fb_free],
                        inc="dve")
                    t_st = env.dma("scalar", rr(S_out), xs[:, :, :], [tap],
                                   ch="st")
                    t_stb = env.dma("scalar", rr(S_out_b), obf[:, :, :], [tcb],
                                    ch="st")
                    t_x_free = t_st
                env.flush(f"C{i}_P2b")



def emit_ln_stats_old(env, src, cast2, prow_s, prow_q, extra_first=()):
    tsum = None
    tsq = None
    for c in range(NCH):
        tcl = env.rec("vector", lambda e, c=c: e.tensor_copy(
            cast2[:, c % 2, :], src[:, c, :]),
            (list(extra_first) if c == 0 else []) + ([tsum] if c >= 2 else []),
            inc="dve")
        tsum = env.rec("tensor", lambda e, c=c: e.matmul(
            prow_s.h[:1, :], env.onescol[:, :], cast2[:, c % 2, :],
            start=(c == 0), stop=(c == NCH - 1)),
            [tcl] + ([prow_s.tok] if c == 0 else []), inc="pe")
    for c in range(NCH):
        tcl = env.rec("vector", lambda e, c=c: e.tensor_tensor(
            cast2[:, c % 2, :], src[:, c, :], src[:, c, :], op=Op.mult),
            [tsq] if c >= 2 else [tsum], inc="dve")
        tsq = env.rec("tensor", lambda e, c=c: e.matmul(
            prow_q.h[:1, :], env.onescol[:, :], cast2[:, c % 2, :],
            start=(c == 0), stop=(c == NCH - 1)),
            [tcl] + ([prow_q.tok] if c == 0 else []), inc="pe")
    return tsum, tsq


def emit_ln_rows_and_apply_old(env, src, dst, rows, nwt_i, prow_s, prow_q, pbc,
                           eps, scol, bcol, t_deps, t_rows_free, t_dst_free,
                           plain=False):
    row = lambda r: rows[0:1, r, :]
    R_MEAN, R_VAR, R_RSTD, R_T1 = 0, 1, 2, 3
    env.rec("vector", lambda e: e.tensor_scalar(
        row(R_MEAN), prow_s.h[:1, :], 1.0 / D, None, op0=Op.mult),
        list(t_deps) + [t_rows_free])
    env.rec("vector", lambda e: e.tensor_tensor(
        row(R_T1), row(R_MEAN), row(R_MEAN), op=Op.mult))
    env.rec("vector", lambda e, eps=eps: e.tensor_scalar(
        row(R_VAR), prow_q.h[:1, :], 1.0 / D, eps, op0=Op.mult, op1=Op.add))
    tvar = env.rec("vector", lambda e: e.tensor_tensor(
        row(R_VAR), row(R_VAR), row(R_T1), op=Op.subtract), inc="dve")
    prow_s.tok = tvar
    prow_q.tok = tvar
    trst = emit_rsqrt(env, row(R_VAR), row(R_RSTD), nwt_i[0:1, :], row(R_T1),
                      inc="dve")
    tb_m = _bcast(env, row(R_MEAN), pbc[0].h[:, :], [trst, pbc[0].tok])
    tb_r = _bcast(env, row(R_RSTD), pbc[1].h[:, :], [trst, pbc[1].tok])
    tsb = None
    for c in range(NCH):
        tsb = env.rec("vector", lambda e, c=c: e.tensor_tensor(
            src[:, c, :], src[:, c, :], pbc[0].h[:, :], op=Op.subtract),
            [tb_m] if c == 0 else (), inc="dve" if c == NCH - 1 else None)
    pbc[0].tok = tsb
    tap = None
    for c in range(NCH):
        if plain:
            tap = env.rec("vector", lambda e, c=c: e.tensor_tensor(
                dst[:, c, :], src[:, c, :], pbc[1].h[:, :], op=Op.mult),
                [tb_r, tsb, t_dst_free] if c == 0 else (),
                inc="dve" if c == NCH - 1 else None)
        else:
            tap = env.rec("vector", lambda e, c=c: e.scalar_tensor_tensor(
                dst[:, c, :], src[:, c, :], scol(c), pbc[1].h[:, :],
                op0=Op.mult, op1=Op.mult),
                [tb_r, tsb, t_dst_free] if c == 0 else ())
            tap = env.rec("vector", lambda e, c=c: e.tensor_scalar(
                dst[:, c, :], dst[:, c, :], bcol(c), None, op0=Op.add),
                inc="dve" if c == NCH - 1 else None)
    pbc[1].tok = tap
    return tap, tb_r



def emit_stageD(env, s_x2="S1", s_x8="S2", s_x3="S0"):
    nc = env.nc
    NTOK = NSEQ * T
    NSLC = NTOK // SL
    SLAB = 512
    NSLAB = NTOK // SLAB
    S1 = env.stream(s_x2)
    S2 = env.stream(s_x8)
    S0 = env.stream(s_x3)
    out = env.dout("out", [NSEQ, T, D])

    with ExitStack() as st:
        sb = lambda nm, shp, dt=F32: st.enter_context(nc.sbuf_tensor(nm, shp, dt))
        a = sb("d0_a", [128, NCH, SL], F32)
        b = sb("d0_b", [128, NCH, SL], F32)
        t_free = None
        t_st = None
        for s in range(NSLC):
            n, q0 = divmod(s * SL, T)
            rr = lambda S: S[n].rearrange("(c p) t -> p c t", p=128)[:, :, q0:q0 + SL]
            tld = env.dma("sync", a[:, :, :], rr(S1), [t_st], ch="ld")
            tldb = env.dma("sync", b[:, :, :], rr(S2), [t_free], ch="ld")
            tadd = None
            for c in range(NCH):
                tadd = env.rec("vector", lambda e, c=c: e.tensor_tensor(
                    a[:, c, :], a[:, c, :], b[:, c, :], op=Op.add),
                    [tld, tldb] if c == 0 else (),
                    inc="dve" if c == NCH - 1 else None)
            t_free = tadd
            t_st = env.dma("scalar", rr(S0), a[:, :, :], [tadd], ch="st")
        env.flush("D0")

    NSUB = SLAB // SL
    with ExitStack() as st:
        sb = lambda nm, shp, dt=F32: st.enter_context(nc.sbuf_tensor(nm, shp, dt))
        ps = lambda nm, shp, dt=F32: st.enter_context(nc.psum_tensor(nm, shp, dt))
        w1_sb = sb("d_w1s", [128, NCH, 2048], BF16)
        w2_sb = sb("d_w2s", [128, NCH, 2048], BF16)
        w3_sb = sb("d_w3s", [128, 16, D], BF16)
        wg_sb = sb("d_wgs", [128, NCH, 8], F32)
        ysl = sb("d_y", [128, NCH, SLAB], BF16)
        gbc = sb("d_gbc", [128, 8, SLAB], BF16)
        res = sb("d_res", [128, NCH, SLAB], F32)
        xsl = sb("d_x", [128, NCH, SL], F32)
        hid = sb("d_hid", [128, 16, SL], BF16)
        tmp = sb("d_tmp", [128, SL], F32)
        tmp2 = sb("d_tmp2", [128, SL], F32)
        s2b = sb("d_s2b", [128, 2, SL], BF16)
        rows = sb("d_rows", [1, 5, SL], F32)
        nwt_i = sb("d_nwi", [1, SL], I32)
        scf = sb("d_scf", [128, SL], F32)
        g_tok = sb("d_gtok", [128, 4, 8], F32)
        gq = sb("d_gq", [128, 4, 8], F32)
        gsc = sb("d_gsc", [128, 8], F32)
        gmsk = sb("d_gmsk", [128, 3, 8], F32)
        grep = sb("d_grep", [128, 128], F32)
        ot = sb("d_ot", [128, 4, D], F32)
        ph = [Bank(ps(f"d_ph{i}", [128, SL], F32)) for i in range(2)]
        pw = [Bank(ps(f"d_pw{i}", [128, SL], F32)) for i in range(2)]
        prow_s = Bank(ps("d_prs", [1, SL], F32))
        prow_q = Bank(ps("d_prq", [1, SL], F32))
        pbc = [Bank(ps(f"d_pbc{i}", [128, SL], F32)) for i in range(2)]

        wg_raw = env.din("d_wg", [D, 8])
        for k in range(NCH):
            env.dma("sync", wg_sb[:, k, :], wg_raw[k * 128:(k + 1) * 128, :])
        t_wg = env.tok("ld")

        def wload(dst, src_name, e, tok, nch=16):
            ts_ = None
            for k in range(nch):
                ts_ = env.dma("sync", dst[:, k, :],
                              env.dram["bf_" + src_name]
                              [(e * nch + k) * 128:(e * nch + k + 1) * 128, :],
                              [tok, env.conv_tok.get(src_name)] if k == 0 else (),
                              ch="ld")
            return ts_

        t_w1 = wload(w1_sb, "d_w1", 0, None, nch=NCH)
        t_w2 = wload(w2_sb, "d_w2", 0, None, nch=NCH)
        t_w3 = wload(w3_sb, "d_w3", 0, None, nch=16)

        t_rows_free = None
        t_y_free = None
        t_gbc_free = None
        t_res_free = None
        t_x_free = None
        t_hid_free = None
        t_st = None
        for sl in range(NSLAB):
            tgbc = None
            tsc_fm = {}
            for sub in range(NSUB):
                s = sl * NSUB + sub
                n, q0 = divmod(s * SL, T)
                tld = env.dma("sync", xsl[:, :, :],
                              S0[n].rearrange("(c p) t -> p c t", p=128)
                              [:, :, q0:q0 + SL], [t_x_free], ch="ld")
                tsum, tsq = emit_ln_stats_old(env, xsl, s2b, prow_s, prow_q, [tld])
                tap, t_rows_free = emit_ln_rows_and_apply_old(
                    env, xsl, xsl, rows, nwt_i, prow_s, prow_q, pbc,
                    LN_EPS, lambda c: None, lambda c: None,
                    [tsum, tsq], t_rows_free, t_y_free, plain=True)
                tyc = None
                for c in range(NCH):
                    tyc = env.rec("vector", lambda e, c=c, sub=sub: e.tensor_copy(
                        ysl[:, c, sub * SL:(sub + 1) * SL], xsl[:, c, :]),
                        [tap, t_y_free] if c == 0 else (),
                        inc="dve" if c == NCH - 1 else None)
                t_x_free = tyc
                tmm = None
                for k in range(NCH):
                    tmm = env.rec("tensor", lambda e, k=k: e.matmul(
                        pbc[0].h[0:8, :], wg_sb[:, k, :], xsl[:, k, :],
                        start=(k == 0), stop=(k == NCH - 1)),
                        [tap, pbc[0].tok, t_wg if (sl == 0 and sub == 0) else None]
                        if k == 0 else (), inc="pe" if k == NCH - 1 else None)
                tsc = env.rec("scalar", lambda e: e.copy(
                    scf[0:8, :], pbc[0].h[0:8, :]), [tmm, tsc_fm.get(sub - 1)],
                    inc="act")
                pbc[0].tok = tsc
                tts = None
                for j in range(4):
                    tts = env.rec("tensor", lambda e, j=j: e.transpose(
                        pbc[1].h[:, j * 128:(j + 1) * 128],
                        scf[:, j * 128:(j + 1) * 128], env.identf[:, :]),
                        [tsc, pbc[1].tok] if j == 0 else (), inc="pe")
                tgt = env.rec("vector", lambda e: e.tensor_copy(
                    g_tok[:, :, :],
                    pbc[1].h[:, :].rearrange("p (j x) -> p j x", x=128)[:, :, 0:8]),
                    [tts], inc="dve")
                pbc[1].tok = tgt
                tsc_fm[sub] = tgt
                tgq = tgt
                for j in range(4):
                    gj = g_tok[:, j, :]
                    gm1 = gmsk[:, 0, :]
                    gm2 = gmsk[:, 1, :]
                    gr = gmsk[:, 2, :]
                    seq = [
                        ("vector", lambda e, gj=gj: e.reduce_max(
                            gsc[:, 0:1], gj, axis=AX)),
                        ("vector", lambda e, gj=gj, gm1=gm1: e.tensor_scalar(
                            gm1, gj, gsc[:, 0:1], None, op0=Op.is_ge)),
                        ("vector", lambda e, gj=gj, gm1=gm1, gr=gr:
                            e.scalar_tensor_tensor(gr, gm1, -1e30, gj,
                                                   op0=Op.mult, op1=Op.add)),
                        ("vector", lambda e, gr=gr: e.reduce_max(
                            gsc[:, 1:2], gr, axis=AX)),
                        ("vector", lambda e, gr=gr, gm2=gm2: e.tensor_scalar(
                            gm2, gr, gsc[:, 1:2], None, op0=Op.is_ge)),
                        ("vector", lambda e: e.tensor_tensor(
                            gsc[:, 2:3], gsc[:, 1:2], gsc[:, 0:1],
                            op=Op.subtract)),
                        ("scalar", lambda e: e.activation(
                            out=gsc[:, 3:4], in_=gsc[:, 2:3], func=AF.Exp)),
                        ("vector", lambda e: e.tensor_scalar(
                            gsc[:, 4:5], gsc[:, 3:4], 1.0, None, op0=Op.add)),
                        ("vector", lambda e: e.reciprocal(
                            gsc[:, 5:6], gsc[:, 4:5])),
                        ("vector", lambda e: e.tensor_tensor(
                            gsc[:, 6:7], gsc[:, 3:4], gsc[:, 5:6], op=Op.mult)),
                        ("vector", lambda e, gm1=gm1: e.tensor_scalar(
                            gm1, gm1, gsc[:, 5:6], None, op0=Op.mult)),
                        ("vector", lambda e, gm2=gm2: e.tensor_scalar(
                            gm2, gm2, gsc[:, 6:7], None, op0=Op.mult)),
                        ("vector", lambda e, gm1=gm1, gm2=gm2, j=j:
                            e.tensor_tensor(gq[:, j, :], gm1, gm2, op=Op.add)),
                    ]
                    for eng, fn in seq:
                        tgq = env.rec(eng, fn, [tgq],
                                      inc="dve" if eng == "vector" else "act")
                for e_ in range(8):
                    ttj = None
                    for j in range(4):
                        trp = env.rec("vector", lambda e, e_=e_, j=j: e.tensor_copy(
                            grep[:, :],
                            gq[:, j, e_:e_ + 1].broadcast_to((128, 128))),
                            [tgq if (e_ == 0 and j == 0) else None,
                             pbc[1].tok if j == 0 else None,
                             ttj if j > 0 else None,
                             t_gbc_free if (e_ == 0 and j == 0) else None],
                            inc="dve")
                        ttj = env.rec("tensor", lambda e, j=j: e.transpose(
                            pbc[1].h[:, j * 128:(j + 1) * 128], grep[:, :],
                            env.identf[:, :]), [trp], inc="pe")
                    tgbc = env.rec("vector",
                                   lambda e, e_=e_, sub=sub: e.tensor_copy(
                                       gbc[:, e_, sub * SL:(sub + 1) * SL],
                                       pbc[1].h[:, :]), [ttj], inc="dve")
                    pbc[1].tok = tgbc
            tlast_ev = None
            t_hid_war = t_hid_free
            for e_ in range(8):
                t_h1mm_last = None
                t_h2mm_last = None
                t_w3mm_last = None
                for sub in range(NSUB):
                    toff = sub * SL
                    thid = None
                    for fc in range(16):
                        t1mm = None
                        for k in range(NCH):
                            t1mm = env.rec(
                                "tensor", lambda e, k=k, fc=fc, toff=toff: e.matmul(
                                    ph[0].h[:, :],
                                    w1_sb[:, k, fc * 128:(fc + 1) * 128],
                                    ysl[:, k, toff:toff + SL], start=(k == 0),
                                    stop=(k == NCH - 1)),
                                ([t_w1, tgbc]
                                 if (fc == 0 and sub == 0) else [])
                                + [ph[0].tok] if k == 0 else (),
                                inc="pe" if k == NCH - 1 else None)
                        t2mm = None
                        for k in range(NCH):
                            t2mm = env.rec(
                                "tensor", lambda e, k=k, fc=fc, toff=toff: e.matmul(
                                    ph[1].h[:, :],
                                    w2_sb[:, k, fc * 128:(fc + 1) * 128],
                                    ysl[:, k, toff:toff + SL], start=(k == 0),
                                    stop=(k == NCH - 1)),
                                ([t_w2] if (fc == 0 and sub == 0) else [])
                                + [ph[1].tok] if k == 0 else (),
                                inc="pe" if k == NCH - 1 else None)
                        if fc == 15 and sub == NSUB - 1:
                            t_h1mm_last = t1mm
                            t_h2mm_last = t2mm
                        tsl = env.rec("scalar", lambda e: e.activation(
                            out=tmp[:, :], in_=ph[0].h[:, :], func=AF.Silu),
                            [t1mm, thid], inc="act")
                        ph[0].tok = tsl
                        thid = env.rec("vector", lambda e, fc=fc: e.tensor_tensor(
                            hid[:, fc, :], tmp[:, :], ph[1].h[:, :], op=Op.mult),
                            [tsl, t2mm] + ([t_hid_war] if fc == 0 else []),
                            inc="dve")
                        ph[1].tok = thid
                    for m in range(NCH):
                        bank = pw[m % 2]
                        tmm = None
                        for k in range(16):
                            tmm = env.rec(
                                "tensor", lambda e, bank=bank, k=k, m=m: e.matmul(
                                    bank.h[:, :],
                                    w3_sb[:, k, m * 128:(m + 1) * 128],
                                    hid[:, k, :], start=(k == 0), stop=(k == 15)),
                                ([thid, t_w3 if (m == 0 and sub == 0) else None]
                                 if m == 0 else []) + [bank.tok]
                                if k == 0 else (),
                                inc="pe" if k == 15 else None)
                        if m == NCH - 1:
                            t_w3mm_last = tmm
                            t_hid_war = tmm
                        tg = env.rec("vector",
                                     lambda e, bank=bank, m=m, e_=e_, toff=toff:
                                     e.tensor_tensor(
                                         tmp2[:, :], bank.h[:, :],
                                         gbc[:, e_, toff:toff + SL], op=Op.mult),
                                     [tmm] + ([t_res_free]
                                              if (e_ == 0 and m == 0 and sub == 0)
                                              else []), inc="dve")
                        if e_ == 0:
                            tlast_ev = env.rec(
                                "vector", lambda e, m=m, toff=toff: e.tensor_copy(
                                    res[:, m, toff:toff + SL], tmp2[:, :]),
                                [tg], inc="dve")
                        else:
                            tlast_ev = env.rec(
                                "vector", lambda e, m=m, toff=toff: e.tensor_tensor(
                                    res[:, m, toff:toff + SL],
                                    res[:, m, toff:toff + SL], tmp2[:, :],
                                    op=Op.add), [tg], inc="dve")
                        bank.tok = tlast_ev
                ne = e_ + 1 if e_ + 1 < 8 else (0 if sl + 1 < NSLAB else None)
                if ne is not None:
                    t_w1 = wload(w1_sb, "d_w1", ne, t_h1mm_last, nch=NCH)
                    t_w2 = wload(w2_sb, "d_w2", ne, t_h2mm_last, nch=NCH)
                    t_w3 = wload(w3_sb, "d_w3", ne, t_w3mm_last, nch=16)
            t_hid_free = t_w3mm_last
            t_y_free = t_h2mm_last
            t_gbc_free = tlast_ev
            for sub in range(NSUB):
                s = sl * NSUB + sub
                n, q0 = divmod(s * SL, T)
                tld = env.dma("sync", xsl[:, :, :],
                              S0[n].rearrange("(c p) t -> p c t", p=128)
                              [:, :, q0:q0 + SL], [t_x_free], ch="ld")
                tfin = None
                for c in range(NCH):
                    tfin = env.rec("vector", lambda e, c=c, sub=sub: e.tensor_tensor(
                        xsl[:, c, :], xsl[:, c, :],
                        res[:, c, sub * SL:(sub + 1) * SL], op=Op.add),
                        [tld, tlast_ev] if c == 0 else (),
                        inc="dve" if c == NCH - 1 else None)
                t_res_free = tfin
                tot = None
                for c in range(NCH):
                    bank = pbc[c % 2]
                    tt = None
                    for f in range(4):
                        tt = env.rec("tensor", lambda e, bank=bank, c=c, f=f:
                                     e.transpose(
                                         bank.h[:, f * 128:(f + 1) * 128],
                                         xsl[:, c, f * 128:(f + 1) * 128],
                                         env.identf[:, :]),
                                     [tfin, bank.tok] if f == 0 else (),
                                     inc="pe" if f == 3 else None)
                    tot = env.rec("vector", lambda e, bank=bank, c=c: e.tensor_copy(
                        ot[:, :, c * 128:(c + 1) * 128],
                        bank.h[:, :].rearrange("p (f x) -> p f x", x=128)),
                        [tt, t_st if c <= 1 else None], inc="dve")
                    bank.tok = tot
                t_x_free = tot
                t_st = env.dma("scalar",
                               out[n, q0:q0 + SL, :].rearrange(
                                   "(f p) d -> p f d", p=128),
                               ot[:, :, :], [tot], ch="st")
        env.flush("D")


def make_arange2d():
    p = np.arange(128, dtype=np.float32)[:, None]
    j = np.arange(SL, dtype=np.float32)[None, :]
    return np.ascontiguousarray(p - j)


def prep_host(inputs):
    h = {}
    h["arange2d"] = make_arange2d()
    h["winT"] = np.ascontiguousarray(inputs["L0_attn_input_weights"].T)
    h["woutT"] = np.ascontiguousarray(inputs["L0_attn_output_weight"].T)
    h["b_wq"] = np.ascontiguousarray(inputs["L0_ffn_Wq"])
    h["b_wdown"] = np.ascontiguousarray(inputs["L0_ffn_Wdown"])
    h["b_wup"] = np.ascontiguousarray(inputs["L0_ffn_Wup"])
    h["b_wo"] = np.ascontiguousarray(inputs["L0_ffn_Wo"])
    h["b_wg"] = np.ascontiguousarray(inputs["L0_ffn_Wg"])
    h["b_we"] = np.ascontiguousarray(inputs["L0_ffn_We"]).reshape(8 * 1024, 1024)
    h["c_wq"] = np.ascontiguousarray(inputs["L1_attn_Wq"]).reshape(8 * 1024, 1024)
    h["c_wk"] = np.ascontiguousarray(inputs["L1_attn_Wk"]).reshape(8 * 1024, 1024)
    h["c_wv"] = np.ascontiguousarray(inputs["L1_attn_Wv"]).reshape(8 * 1024, 1024)
    h["c_wo"] = np.ascontiguousarray(inputs["L1_attn_Wo"]).reshape(8 * 1024, 1024)
    h["c_wf1"] = np.ascontiguousarray(inputs["L1_attn_Wf1"]).reshape(8 * 1024, 4096)
    h["c_wf2"] = np.ascontiguousarray(inputs["L1_attn_Wf2"]).reshape(8 * 4096, 1024)
    h["d_wg"] = np.ascontiguousarray(inputs["L1_ffn_Wg"].T)
    h["d_w1"] = np.ascontiguousarray(
        inputs["L1_ffn_W1"].transpose(0, 2, 1)).reshape(8 * 1024, 2048)
    h["d_w2"] = np.ascontiguousarray(
        inputs["L1_ffn_W2"].transpose(0, 2, 1)).reshape(8 * 1024, 2048)
    h["d_w3"] = np.ascontiguousarray(
        inputs["L1_ffn_W3"].transpose(0, 2, 1)).reshape(8 * 2048, 1024)
    h["vecs"] = pack_vecs(inputs)
    return h


def _vec_specs():
    specs = [
        ("b_g1", "L0_ffn_gamma1", None, 1024),
        ("b_bq", "L0_ffn_bq", None, 1024),
        ("b_bdown", "L0_ffn_bdown", None, 64),
        ("b_bup", "L0_ffn_bup", None, 2048),
        ("b_bo", "L0_ffn_bo", None, 1024),
        ("b_g2", "L0_ffn_gamma2", None, 1024),
        ("b_bg", "L0_ffn_bg", None, 8),
    ]
    for i in range(8):
        specs += [
            (f"c_bq{i}", "L1_attn_bq", i, 1024),
            (f"c_bk{i}", "L1_attn_bk", i, 1024),
            (f"c_bv{i}", "L1_attn_bv", i, 1024),
            (f"c_bo{i}", "L1_attn_bo", i, 1024),
            (f"c_l1s{i}", "L1_attn_ln1_s", i, 1024),
            (f"c_l1b{i}", "L1_attn_ln1_b", i, 1024),
            (f"c_bf1_{i}", "L1_attn_bf1", i, 4096),
            (f"c_bf2_{i}", "L1_attn_bf2", i, 1024),
            (f"c_l2s{i}", "L1_attn_ln2_s", i, 1024),
            (f"c_l2b{i}", "L1_attn_ln2_b", i, 1024),
        ]
    return specs

VEC_SPECS = _vec_specs()
VIDX = {}
_r = 0
for _nm, _k, _i, _L in VEC_SPECS:
    VIDX[_nm] = _r
    _r += (_L + 127) // 128
VROWS = ((_r + 127) // 128) * 128


def pack_vecs(inputs):
    arr = np.zeros((VROWS, 128), np.float32)
    for nm, key, i, L in VEC_SPECS:
        vec = np.asarray(inputs[key] if i is None else inputs[key][i],
                         np.float32).reshape(-1)
        assert len(vec) == L, (nm, len(vec), L)
        r = VIDX[nm]
        n = (L + 127) // 128
        pad = np.zeros(n * 128, np.float32)
        pad[:L] = vec
        arr[r:r + n] = pad.reshape(n, 128)
    return np.ascontiguousarray(arr)



def build(parts=("conv", "A"), stream_kinds=None, c_blocks=None, c_dev_out=None):
    parts = set(parts)
    if "B" in parts:
        parts |= {"B1", "B2"}
    nc = bass.Bass("TRN2", target_bir_lowering=False)
    env = Env(nc, stream_kinds)
    with env.stack:
        emit_globals(env)
        conv = []
        for p in ("A", "B1", "B2", "C", "D"):
            if p in parts:
                conv += CONV_NEED[p]
        if conv:
            emit_convert(env, conv)
        if "A" in parts:
            emit_stageA(env)
        if "B1" in parts:
            emit_stageB1(env)
        if "B2" in parts:
            emit_stageB2(env)
        if "C" in parts:
            emit_stageC(env, blocks=c_blocks or tuple(range(8)),
                        dev_out=c_dev_out)
        if "D" in parts:
            emit_stageD(env)
        emit_finalize(env)
    return env


def emit_finalize(env):
    waits = [env.tok(ch) for ch in ("st", "cv_st") if env.cnt.get(ch, 0) > 0]
    if waits:
        env.wait_only("sync", waits)
    env.flush("finalize")


def kernel(**inputs):
    inputs = {k: np.asarray(v) for k, v in inputs.items()}
    h = prep_host(inputs)
    env = build(parts=("A", "B", "C", "D"))
    x0 = inputs["L0_attn_values"]
    shared = {}
    for name in env.needed_inputs:
        if name != "x0":
            shared[name] = np.ascontiguousarray(h[name])
    in_maps = []
    for core in range(NCORE):
        m = dict(shared)
        m["x0"] = np.ascontiguousarray(x0[core * NSEQ:(core + 1) * NSEQ])
        in_maps.append(m)
    res = run_bass_kernel_spmd(env.nc, in_maps, core_ids=list(range(NCORE)))
    out = np.concatenate([res.results[c]["out"] for c in range(NCORE)], axis=0)
    return out.astype(np.float32)
